# version 10
# baseline (speedup 1.0000x reference)
import math
import numpy as np
import ml_dtypes
from contextlib import ExitStack

import concourse.bass as bass
import concourse.mybir as mybir
from concourse.bass_utils import run_bass_kernel_spmd

F32 = mybir.dt.float32
BF16 = mybir.dt.bfloat16
I32 = mybir.dt.int32
AF = mybir.ActivationFunctionType
OP = mybir.AluOpType
AX = mybir.AxisListType
BF = ml_dtypes.bfloat16

S, D, NH, DN, DR, DV, C, I = 4096, 2048, 16, 128, 64, 128, 512, 8192
NT, TT = 8, 512
SCALE = 1.0 / math.sqrt(DN + DR)
LN_EPS = 1e-5
RMS_EPS = 1.1920928955078125e-07
MHAT_BIAS = math.log(SCALE) + 0.25 - 18.0

import os as _os
SUBJ = _os.environ.get("SUBJ", "")
ENGS = ("sync", "tensor", "scalar", "vector", "gpsimd")
OWNSEM = {"tensor": "pe", "scalar": "act", "vector": "dve", "gpsimd": "pool"}
JSEM = {"tensor": "jpe", "scalar": "jact", "vector": "jdve"}


def _host_tables():
    t = {}
    f = (10000.0 ** (-np.arange(0, DR, 2, dtype=np.float64) / DR))
    ang = np.arange(S, dtype=np.float64)[:, None] * f[None, :]
    cos, sin = np.cos(ang), np.sin(ang)
    cosq = np.empty((128, S), np.float32)
    sgnq = np.empty((128, S), np.float32)
    for r in range(128):
        j = r % 32
        cosq[r] = cos[:, j]
        sgnq[r] = (-sin[:, j]) if (r % 64) < 32 else sin[:, j]
    t["COSQ"] = cosq
    t["SGNQ"] = sgnq
    kt = np.empty((S, 64), np.float32)
    kt[:, 0:32] = cos
    kt[:, 32:64] = sin
    t["KTAB"] = kt
    t["IDT"] = np.eye(128, dtype=BF)
    t["TRI"] = np.triu(np.ones((128, 128), np.float32)).astype(BF)
    pswp = np.zeros((128, 128), np.float32)
    for r in range(128):
        pswp[r, r ^ 32] = 1.0
    t["PSWP"] = pswp.astype(BF)
    ehq = np.zeros((128, 24 * 16), np.float32)
    for oc in range(16):
        ehq[:, oc * 16 + oc] = 1.0
    for pc in range(8):
        oc = 16 + pc
        ehq[0:64, oc * 16 + 2 * pc] = 1.0
        ehq[64:128, oc * 16 + 2 * pc + 1] = 1.0
    t["EHQ"] = ehq.astype(BF)
    t["ONES64"] = np.ones((64, 1), np.float32).astype(BF)
    t["ONES1_16"] = np.ones((1, 16), np.float32).astype(BF)
    t["ONES1_128"] = np.ones((1, 128), np.float32).astype(BF)
    cst = np.zeros((128, 8), np.float32)
    cst[:, 0] = LN_EPS
    cst[:, 1] = -0.5
    cst[:, 2] = 1.0 / 512
    cst[:, 3] = RMS_EPS
    cst[:, 4] = 0.5
    cst[:, 5] = MHAT_BIAS
    cst[:, 6] = SCALE
    cst[:, 7] = -1.0
    t["CONSTS"] = cst
    return t


def _host_weights(inp):
    h = {}
    Wq = np.asarray(inp["L0_attn_Wq"])
    perm = []
    for hh in range(NH):
        perm += [hh * 192 + j for j in range(128)]
    for pc in range(8):
        for hh in (2 * pc, 2 * pc + 1):
            perm += [hh * 192 + 128 + 2 * j for j in range(32)]
            perm += [hh * 192 + 128 + 2 * j + 1 for j in range(32)]
    A = Wq[:, perm].reshape(16, 128, 24, 128)
    h["h_wq"] = np.ascontiguousarray(A.transpose(2, 1, 0, 3).reshape(24, 128, 2048))

    Wkva = np.asarray(inp["L0_attn_Wkv_a"])
    kperm = list(range(512)) + [512 + 2 * j for j in range(32)] + [512 + 2 * j + 1 for j in range(32)]
    B = Wkva[:, kperm].reshape(16, 128, 576)
    h["h_wkva"] = np.ascontiguousarray(B.transpose(1, 0, 2).reshape(128, 16 * 576))

    Wkvb = np.asarray(inp["L0_attn_Wkv_b"])
    kn_cols = [hh * 256 + j for hh in range(NH) for j in range(128)]
    v_cols = [hh * 256 + 128 + j for hh in range(NH) for j in range(128)]
    Bk = Wkvb[:, kn_cols].reshape(4, 128, 16, 128).transpose(2, 1, 0, 3)
    h["h_kbn"] = np.ascontiguousarray(Bk.reshape(16, 128, 512).reshape(4, 4, 128, 512)
                                      .transpose(0, 2, 1, 3).reshape(4, 128, 2048))
    h["h_kbv"] = np.ascontiguousarray(Wkvb[:, v_cols].reshape(4, 128, 2048))

    Wo = np.asarray(inp["L0_attn_Wo"])
    h["h_wo"] = np.ascontiguousarray(Wo.reshape(16, 128, 2048))

    gw = np.asarray(inp["L0_ffn_gate_w"])
    uw = np.asarray(inp["L0_ffn_up_w"])
    dw = np.asarray(inp["L0_ffn_down_w"])
    G = gw.reshape(64, 128, 16, 128)
    h["h_g"] = np.ascontiguousarray(G.transpose(0, 3, 2, 1).reshape(64, 128, 2048))
    U = uw.reshape(64, 128, 16, 128)
    h["h_u"] = np.ascontiguousarray(U.transpose(0, 3, 2, 1).reshape(64, 128, 2048))
    Dn = dw.reshape(2048, 64, 128).transpose(1, 2, 0)
    h["h_d"] = np.ascontiguousarray(Dn)

    h["gsc"] = np.ascontiguousarray(np.asarray(inp["L0_ffn_gate_scale"]).reshape(64, 128).T)
    h["usc"] = np.ascontiguousarray(np.asarray(inp["L0_ffn_up_scale"]).reshape(64, 128).T)
    h["dscrep"] = np.ascontiguousarray(np.tile(np.asarray(inp["L0_ffn_down_scale"])[None, :], (128, 1)))
    h["wrep"] = np.ascontiguousarray(np.tile(np.asarray(inp["L0_attn_kv_norm_w"])[None, :], (128, 1)))
    return h


class Prog:
    def __init__(self, nc, stack):
        self.nc = nc
        self.q = {e: [] for e in ENGS}
        self.sems = {}
        self.cnt = {}
        self.hi = {e: {} for e in ENGS}
        self.pending = {}
        names = ["cv", "cvo", "pe", "act", "dve", "pool", "jpe", "jact", "jdve",
                 "tab0", "tab", "wkv", "krb", "krs"]
        for pfx in ("xb", "kt", "wq", "kbb", "vs", "kn", "vb",
                    "gb", "ub", "wd", "wo", "xr", "ks", "vt", "ot"):
            names += [pfx + "0", pfx + "1"]
        names += ["ot2", "ot3", "wd2"]
        for name in names:
            self.sems[name] = stack.enter_context(nc.semaphore(name))
            self.cnt[name] = 0

    def emit(self, eng, fn, waits=(), inc=None, inc_n=1, is_dma=False, fast=False):
        ws = []
        hw = self.hi[eng]
        for s, v in list(self.pending.pop(eng, [])) + list(waits):
            if v is None or v <= 0:
                continue
            if hw.get(s, -1) >= v:
                continue
            hw[s] = v
            ws.append((s, v))
        tok = None
        incs = []
        if inc is not None:
            self.cnt[inc] += inc_n
            tok = self.cnt[inc]
            incs.append((inc, inc_n))
        self.q[eng].append([fn, ws, incs, is_dma or fast])
        return tok

    def dma(self, eng, sem, out_ap, in_ap, waits=(), **kw):
        fn = (lambda e, o=out_ap, i=in_ap, kw=kw: e.dma_start(o, i, **kw))
        return self.emit(eng, fn, waits, inc=sem, inc_n=16, is_dma=True)

    def junction(self):
        for eng, js in JSEM.items():
            if self.q[eng]:
                self.emit(eng, lambda e: e.drain(), inc=js, is_dma=True)
        snap = [(s, self.cnt[s]) for s in ("jpe", "jact", "jdve")]
        for eng in ("sync", "tensor", "scalar", "vector"):
            self.pending[eng] = list(snap)


def build_program():
    nc = bass.Bass("TRN2", target_bir_lowering=False)
    stack = ExitStack()
    P = Prog(nc, stack)
    d = {}
    sb = {}

    def dram(name, shape, dt, kind="ExternalInput"):
        d[name] = nc.dram_tensor(name, list(shape), dt, kind=kind).ap()
        return d[name]

    def sbuf(name, shape, dt):
        sb[name] = stack.enter_context(nc.sbuf_tensor(name, list(shape), dt))
        return sb[name]

    dram("x", (S, D), F32)
    dram("h_wq", (24, 128, 2048), F32)
    dram("h_wkva", (128, 16 * 576), F32)
    dram("h_kbn", (4, 128, 2048), F32)
    dram("h_kbv", (4, 128, 2048), F32)
    dram("h_wo", (16, 128, 2048), F32)
    dram("h_g", (64, 128, 2048), I32)
    dram("h_u", (64, 128, 2048), I32)
    dram("h_d", (64, 128, 2048), I32)
    dram("gsc", (128, 64), F32)
    dram("usc", (128, 64), F32)
    dram("dscrep", (128, 2048), F32)
    dram("wrep", (128, 512), F32)
    dram("COSQ", (128, S), F32)
    dram("SGNQ", (128, S), F32)
    dram("KTAB", (S, 64), F32)
    dram("IDT", (128, 128), BF16)
    dram("TRI", (128, 128), BF16)
    dram("PSWP", (128, 128), BF16)
    dram("EHQ", (128, 384), BF16)
    dram("ONES64", (64, 1), BF16)
    dram("ONES1_16", (1, 16), BF16)
    dram("ONES1_128", (1, 128), BF16)
    dram("CONSTS", (128, 8), F32)
    dram("out", (S, D), F32, kind="ExternalOutput")
    dram("s_wq", (24, 128, 2048), BF16, kind="Internal")
    dram("s_wkva", (128, 16 * 576), BF16, kind="Internal")
    dram("s_kbn", (4, 128, 2048), BF16, kind="Internal")
    dram("s_kbv", (4, 128, 2048), BF16, kind="Internal")
    dram("s_wo", (2, 16, 128, 1024), BF16, kind="Internal")
    dram("s_g", (64, 128, 2048), BF16, kind="Internal")
    dram("s_u", (64, 128, 2048), BF16, kind="Internal")
    dram("s_wd", (2, 64, 128, 1024), BF16, kind="Internal")
    dram("s_kn", (16, 128, S), BF16, kind="Internal")
    dram("s_kr", (128, S), BF16, kind="Internal")
    dram("s_v", (16, 32, 128, 128), BF16, kind="Internal")

    sbuf("xnT", (128, 16 * 512), BF16)
    sbuf("OT", (128, 16 * 512), BF16)
    sbuf("cvb", (128, 8192), BF16)
    sbuf("dscb", (128, 2048), BF16)
    sbuf("IDTb", (128, 128), BF16)
    sbuf("TRIb", (128, 128), BF16)
    sbuf("PSWPb", (128, 128), BF16)
    sbuf("EHQb", (128, 384), BF16)
    sbuf("o64", (64, 1), BF16)
    sbuf("o116", (1, 16), BF16)
    sbuf("o1128", (1, 128), BF16)
    sbuf("gscb", (128, 64), F32)
    sbuf("uscb", (128, 64), F32)
    sbuf("wrepb", (128, 512), F32)
    sbuf("negm", (128, 16), F32)
    sbuf("krow", (1, 512), BF16)
    sbuf("st16", (16, 16), F32)
    sbuf("nmhb", (16, 1), BF16)
    sbuf("nmrow", (1, 16), BF16)
    sbuf("rl", (128, 2), F32)
    sbuf("cst", (128, 8), F32)
    sbuf("poolb", (128, 47104), BF16)
    sbuf("poolf", (128, 12304), F32)

    pb = [stack.enter_context(nc.psum_tensor(f"pb{i}", [128, 512], F32)) for i in range(6)]
    pbT_g = [nc.psum_tensor("pbT0_0", [128, 1024], BF16), nc.psum_tensor("pbT1_0", [128, 1024], BF16)]
    pbT = [g.__enter__() for g in pbT_g]

    def bview(off, n):
        return sb["poolb"][:, off:off + n]

    def fview(off, n):
        return sb["poolf"][:, off:off + n]

    QN_O, QR_O, CKT_O = 0, 8192, 12288
    WKVA_O, XNTOK_O, WQB_O, KBB_O = 14336, 23552, 25600, 29696
    SQST_O, KNST_O, KRSQ_O, KRST_O, VST_O, VSL_O = 30720, 31744, 32768, 33280, 33792, 35840
    KRB_O, KNB_O, VB_O, PT_O, OST_O = 14336, 18432, 26624, 34944, 36480
    HBUF_O, GB_O, UB_O, WDB_O, WOB_O, SGST_O, UST_O = 0, 32768, 36864, 40960, 44032, 46080, 46592
    XB_O, XSQ_O, STA_O = 0, 4096, 6144
    CKV_O, T512_O, QR32_O, TMA_O, TMB_O = 6160, 6736, 7248, 7760, 8272
    CSQ_O, SGQ_O, KT_O = 8784, 9296, 9808
    XR_O, OUTS_O = 6160, 8208

    E = P.emit
    DMA = P.dma
    sems = P.sems

    for nm, dst in (("IDT", sb["IDTb"]), ("TRI", sb["TRIb"]), ("PSWP", sb["PSWPb"]),
                    ("EHQ", sb["EHQb"]), ("ONES64", sb["o64"]), ("ONES1_16", sb["o116"]),
                    ("ONES1_128", sb["o1128"]), ("gsc", sb["gscb"]), ("usc", sb["uscb"]),
                    ("wrep", sb["wrepb"]), ("CONSTS", sb["cst"])):
        DMA("sync", "tab0", dst[:, :] if dst.shape[0] == 128 else dst[0:dst.shape[0], :],
            d[nm][0:dst.shape[0], :] if dst.shape[0] != 128 else d[nm][:, :])
    tok_tables = P.cnt["tab0"]
    E("vector", lambda e: e.memset(sb["st16"][0:16, 2:3], 0.0), inc="dve")
    tok_init_dve = P.cnt["dve"]

    cv_half = [sb["cvb"][:, i * 2048:(i + 1) * 2048] for i in range(4)]
    war = {0: 0, 1: 0, 2: 0, 3: 0}
    pool_war = {0: 0, 1: 0, 2: 0, 3: 0}
    mark = {}

    DMA("gpsimd", "cv", sb["dscb"][:, :], d["dscrep"][:, :])

    def conv_f32(src, dsts, p, n=2048):
        t1 = DMA("gpsimd", "cv", cv_half[p][:, 0:n], src, waits=[("cvo", war[p])])
        tok = None
        off = 0
        for ds, w in dsts:
            tok = DMA("gpsimd", "cvo", ds, cv_half[p][:, off:off + w], waits=[("cv", t1)])
            off += w
        war[p] = tok
        return tok

    def conv_int(src, dsts, p):
        tm = E("gpsimd", lambda e, a=cv_half[p]: e.memset(a, -128.0),
               waits=[("cvo", war[p])], inc="pool")
        t1 = DMA("gpsimd", "cv", cv_half[p], src, waits=[("pool", tm)], accum_op=OP.add)
        tok = None
        off = 0
        for ds, w in dsts:
            tok = DMA("gpsimd", "cvo", ds, cv_half[p][:, off:off + w], waits=[("cv", t1)])
            off += w
        war[p] = tok
        return tok

    def conv_wd(ic, p):
        tm = E("gpsimd", lambda e, a=cv_half[p]: e.memset(a, -128.0),
               waits=[("pool", pool_war[p]), ("cvo", war[p])], inc="pool")
        t1 = DMA("gpsimd", "cv", cv_half[p], d["h_d"][ic], waits=[("pool", tm)], accum_op=OP.add)
        tt = E("gpsimd", lambda e, o=cv_half[2 + p], a=cv_half[p]: e.tensor_mul(o, a, sb["dscb"][:, :]),
               waits=[("cv", t1), ("cvo", war[2 + p])], inc="pool")
        pool_war[p] = tt
        tok = None
        for egh in range(2):
            tok = DMA("gpsimd", "cvo", d["s_wd"][egh, ic],
                      cv_half[2 + p][:, egh * 1024:(egh + 1) * 1024], waits=[("pool", tt)])
        war[2 + p] = tok
        return tok

    u = 0
    for k in range(5):
        n = 2048 if k < 4 else 1024
        mark["wkva"] = conv_f32(d["h_wkva"][:, k * 2048:k * 2048 + n],
                                [(d["s_wkva"][:, k * 2048:k * 2048 + n], n)], u % 2, n)
        u += 1
    for oc in range(24):
        mark["wq"] = conv_f32(d["h_wq"][oc], [(d["s_wq"][oc], 2048)], u % 2)
        u += 1
    for g in range(4):
        mark["kb"] = conv_f32(d["h_kbn"][g], [(d["s_kbn"][g], 2048)], u % 2)
        u += 1
    for g in range(4):
        mark["kb"] = conv_f32(d["h_kbv"][g], [(d["s_kbv"][g], 2048)], u % 2)
        u += 1
    for ic in range(64):
        mark[("g", ic)] = conv_int(d["h_g"][ic], [(d["s_g"][ic], 2048)], u % 2); u += 1
        mark[("u", ic)] = conv_int(d["h_u"][ic], [(d["s_u"][ic], 2048)], u % 2); u += 1
    for ic in range(64):
        mark[("wd", ic)] = conv_wd(ic, ic % 2)
    for hh in range(16):
        mark[("wo", hh)] = conv_f32(d["h_wo"][hh],
                                    [(d["s_wo"][0, hh], 1024), (d["s_wo"][1, hh], 1024)], u % 2)
        u += 1
    cvo_final = P.cnt["cvo"]
    for k in list(mark):
        mark[k] = min(mark[k] + 64, cvo_final)

    state = {"pbT": pbT, "pbT_g": pbT_g}

    def mm(out_ap, lhsT, rhs, start, stop, waits=(), inc="pe"):
        return E("tensor",
                 lambda e, o=out_ap, l=lhsT, r=rhs, s1=start, s2=stop: e.matmul(o, l, r, start=s1, stop=s2),
                 [("tab0", tok_tables)] + list(waits), inc=inc)

    def transpose(out_ap, in_ap, waits=()):
        return E("tensor",
                 lambda e, o=out_ap, i=in_ap: e.transpose(o, i, sb["IDTb"][:, :]),
                 [("tab0", tok_tables)] + list(waits), inc="pe")

    def nop(eng, waits=()):
        return E(eng, lambda e: e.nop(cycle_cnt=384), waits, is_dma=True)

    def act(fn, waits=(), fast=False):
        return E("scalar", fn, [("tab0", tok_tables)] + list(waits), inc="act", fast=fast)

    def dve(fn, waits=(), fast=False):
        return E("vector", fn, [("tab0", tok_tables)] + list(waits), inc="dve", fast=fast)

    xnT = [sb["xnT"][:, c * 512:(c + 1) * 512] for c in range(16)]
    OTv = [sb["OT"][:, hh * 512:(hh + 1) * 512] for hh in range(16)]
    stats = sb["st16"]

    for qt in range(NT):
        T0 = qt * TT
        nkc = 4 * (qt + 1)
        kend = T0 + TT

        ocs = state.get(("ot_prev",), (0, 0, 0, 0))
        DMA("sync", "tab", fview(CSQ_O, 512), d["COSQ"][:, T0:kend],
            waits=[("dve", state.get(("evac_prev",), 0))] +
                  [(f"ot{j}", ocs[j]) for j in range(4)])
        tok_sgnq = tok_cosq = DMA("sync", "tab", fview(SGQ_O, 512), d["SGNQ"][:, T0:kend])
        tok_wkva = DMA("sync", "wkv", bview(WKVA_O, 9216), d["s_wkva"][:, :],
                       waits=[("cvo", mark["wkva"])])

        tok_xnT = {}
        tok_ckvnT = {}
        tok_x, tok_kt = [], []
        last_kr_evac = None
        for j in range(4):
            pre = state.pop(("tok_x_pre", qt, j), None)
            tok_x.append(pre if pre is not None else
                         DMA("sync", f"xb{j % 2}", fview(XB_O + (j % 2) * 2048, 2048),
                             d["x"][T0 + j * 128:T0 + (j + 1) * 128, :],
                             waits=[("dve", state.get(("xbfree", j % 2), 0))]))
            tok_kt.append(DMA("sync", f"kt{j % 2}", fview(KT_O + (j % 2) * 64, 64),
                              d["KTAB"][T0 + j * 128:T0 + (j + 1) * 128, :],
                              waits=[("dve", state.get(("ktab_free", j % 2), 0))]))
            xb = fview(XB_O + (j % 2) * 2048, 2048)
            xsq = fview(XSQ_O, 2048)
            st = fview(STA_O, 16)
            tA = act(lambda e, o=xsq, i=xb: e.activation(o, i, AF.Square),
                     waits=[(f"xb{j % 2}", tok_x[j]), ("dve", state.get(("xsq_free",), 0))])
            t1 = dve(lambda e, o=st[:, 0:1], i=xb: e.reduce_sum(o, i, AX.X), waits=[("act", tA)])
            t2 = dve(lambda e, o=st[:, 1:2], i=xsq: e.reduce_sum(o, i, AX.X))
            state[("xsq_free",)] = t2
            dve(lambda e, o=st[:, 2:3], i=st[:, 0:1]: e.tensor_scalar_mul(o, i, 1.0 / 2048))
            dve(lambda e, o=st[:, 3:4], i=st[:, 1:2]: e.tensor_scalar_mul(o, i, 1.0 / 2048))
            dve(lambda e, o=st[:, 4:5], a=st[:, 2:3]: e.tensor_mul(o, a, a))
            t3 = dve(lambda e, o=st[:, 5:6], a=st[:, 3:4], b=st[:, 4:5]: e.tensor_sub(o, a, b))
            nop("scalar", waits=[("dve", t3)])
            act(lambda e, o=st[:, 6:7], i=st[:, 5:6]: e.activation(o, i, AF.Ln, bias=sb["cst"][:, 0:1]))
            tR = act(lambda e, o=st[:, 7:8], i=st[:, 6:7]: e.activation(o, i, AF.Exp, scale=sb["cst"][:, 1:2]))
            tN = dve(lambda e, o=st[:, 8:9], i=st[:, 2:3]: e.tensor_scalar_mul(o, i, -1.0))
            xntok = bview(XNTOK_O, 2048)
            nop("vector", waits=[("act", tR)])
            tXN = dve(lambda e, o=xntok, i=xb, s1=st[:, 8:9], s2=st[:, 7:8]:
                      e.tensor_scalar(o, i, s1, s2, OP.add, OP.mult),
                      waits=[("pe", state.get(("xntok_free",), 0)),
                             ("pe", state.get(("tokR_prev",), 0))])
            state[("xbfree", j % 2)] = tXN
            for c in range(16):
                tp = transpose(state["pbT"][c % 2][:, 0:128], xntok[:, c * 128:(c + 1) * 128],
                               waits=[("dve", tXN if c == 0 else 0),
                                      ("act", 0 if c == 0 else state.get(("pbT_free", c % 2), 0))])
                if c == 15:
                    state[("xntok_free",)] = tp
                te = act(lambda e, o=xnT[c][:, j * 128:(j + 1) * 128], i=state["pbT"][c % 2][:, 0:128]:
                         e.copy(o, i), waits=[("pe", tp)], fast=True)
                state[("pbT_free", c % 2)] = te
                tok_xnT[(j, c)] = te
            w = [("wkv", tok_wkva), ("act", tok_xnT[(j, 15)]),
                 ("dve", state.get(("ckv_free", qt, j - 1), 0))]
            for dc in range(16):
                wv = w if dc == 0 else ()
                mm(pb[2][:, :], xnT[dc][:, j * 128:(j + 1) * 128],
                   bview(WKVA_O + dc * 576, 512), dc == 0, dc == 15, waits=wv, inc=None)
            for dc in range(16):
                tkva = mm(pb[3][:, 0:64], xnT[dc][:, j * 128:(j + 1) * 128],
                          bview(WKVA_O + dc * 576 + 512, 64), dc == 0, dc == 15,
                          inc="pe" if dc == 15 else None)
            ckv = fview(CKV_O, 576)
            ocs = state.get(("ot_prev",), (0, 0, 0, 0))
            tc1 = dve(lambda e, o=ckv[:, 0:512], i=pb[2][:, :]: e.tensor_copy(o, i),
                      waits=[("pe", tkva)] + [(f"ot{j2}", ocs[j2]) for j2 in range(4)])
            tc2 = dve(lambda e, o=ckv[:, 512:576], i=pb[3][:, 0:64]: e.tensor_copy(o, i))
            state[("ckv_free", qt, j)] = tc2
            kt = fview(KT_O + (j % 2) * 64, 64)
            tma, tmb = fview(TMA_O, 512), fview(TMB_O, 512)
            krt = bview(KRST_O, 512)
            dve(lambda e, o=tma[:, 0:32], a=ckv[:, 512:544], b=kt[:, 0:32]: e.tensor_mul(o, a, b),
                waits=[(f"kt{j % 2}", tok_kt[j])])
            dve(lambda e, o=tmb[:, 0:32], a=ckv[:, 544:576], b=kt[:, 32:64]: e.tensor_mul(o, a, b))
            dve(lambda e, o=krt[:, 0:32], a=tma[:, 0:32], b=tmb[:, 0:32]: e.tensor_sub(o, a, b),
                waits=[("pe", state.get(("krt_free",), 0))])
            dve(lambda e, o=tma[:, 0:32], a=ckv[:, 544:576], b=kt[:, 0:32]: e.tensor_mul(o, a, b))
            tktf = dve(lambda e, o=tmb[:, 0:32], a=ckv[:, 512:544], b=kt[:, 32:64]: e.tensor_mul(o, a, b))
            state[("ktab_free", j % 2)] = tktf
            tkr = dve(lambda e, o=krt[:, 32:64], a=tma[:, 0:32], b=tmb[:, 0:32]: e.tensor_add(o, a, b))
            t512 = fview(T512_O, 512)
            dve(lambda e, o=t512, a=ckv[:, 0:512]: e.tensor_mul(o, a, a))
            tss = dve(lambda e, o=st[:, 9:10], i=t512: e.reduce_sum(o, i, AX.X))
            nop("scalar", waits=[("dve", tss)])
            act(lambda e, o=st[:, 10:11], i=st[:, 9:10]:
                e.activation(o, i, AF.Ln, scale=sb["cst"][:, 2:3], bias=sb["cst"][:, 3:4]))
            tr2 = act(lambda e, o=st[:, 11:12], i=st[:, 10:11]:
                      e.activation(o, i, AF.Exp, scale=sb["cst"][:, 1:2]))
            ckvnb = bview(KNST_O, 512)
            ckvnb = bview(SQST_O, 512)
            nop("vector", waits=[("act", tr2)])
            tck = dve(lambda e, o=ckvnb, a=ckv[:, 0:512], s=st[:, 11:12]:
                      e.scalar_tensor_tensor(o, a, s, sb["wrepb"][:, :], OP.mult, OP.mult),
                      waits=[("pe", state.get(("ckvnb_free",), 0))])
            for cc in range(4):
                tp = transpose(state["pbT"][cc % 2][:, 0:128], ckvnb[:, cc * 128:(cc + 1) * 128],
                               waits=[("dve", tck), ("act", state.get(("pbT_free", cc % 2), 0))])
                te = act(lambda e, o=bview(CKT_O + cc * 512, 512)[:, j * 128:(j + 1) * 128],
                         i=state["pbT"][cc % 2][:, 0:128]: e.copy(o, i), waits=[("pe", tp)], fast=True)
                state[("pbT_free", cc % 2)] = te
                tok_ckvnT[(j, cc)] = te
            state[("ckvnb_free",)] = tp
            nop("tensor", waits=[("dve", tkr)])
            tp = transpose(state["pbT"][0][0:64, 0:128], krt[:, 0:64],
                           waits=[("act", state.get(("pbT_free", 0), 0))])
            state[("krt_free",)] = tp
            te = act(lambda e, o=bview(KRSQ_O, 512)[0:64, j * 128:(j + 1) * 128],
                     i=state["pbT"][0][0:64, 0:128]: e.copy(o, i), waits=[("pe", tp)], fast=True)
            state[("pbT_free", 0)] = te
            last_kr_evac = te

        if "a" in SUBJ:
            P.junction()
        krstage = bview(KRSQ_O, 512)
        DMA("scalar", "krs", d["s_kr"][0:64, T0:kend], krstage[0:64, :],
            waits=[("act", last_kr_evac)])
        tok_kr_st = DMA("scalar", "krs", d["s_kr"][64:128, T0:kend], krstage[0:64, :])

        sqst = [bview(KRST_O, 512), bview(VST_O, 512)]
        qn = [bview(QN_O + oc * 512, 512) for oc in range(16)]
        qr = [bview(QR_O + pc * 512, 512) for pc in range(8)]
        tok_sq = {}
        tok_normq = {}
        tok_wq = {}
        for oc in range(24):
            tok_wq[oc] = DMA("sync", f"wq{oc % 2}", bview(WQB_O + (oc % 2) * 2048, 2048), d["s_wq"][oc],
                             waits=[("cvo", mark["wq"]),
                                    ("pe", state.get(("wq_free", qt, oc - 2), 0))])
            bank = pb[oc % 2]
            w = [(f"wq{oc % 2}", tok_wq[oc]), ("act", state.get(("qevac", oc - 2), 0)),
                 ("dve", state.get(("qrope_done", oc - 2), 0))]
            for dc in range(16):
                tq = mm(bank[:, :], bview(WQB_O + (oc % 2) * 2048, 2048)[:, dc * 128:(dc + 1) * 128],
                        xnT[dc], dc == 0, dc == 15, waits=(w if dc == 0 else ()),
                        inc="pe" if dc == 15 else None)
            state[("wq_free", qt, oc)] = tq
            tsq = act(lambda e, o=sqst[oc % 2], i=bank[:, :]: e.activation(o, i, AF.Square),
                      waits=[("pe", tq), ("pe", tok_normq.get(oc - 2, 0))])
            tok_sq[oc] = tsq
            if oc < 16:
                tev = act(lambda e, o=qn[oc], i=bank[:, :]: e.copy(o, i), fast=True)
            else:
                pc = oc - 16
                qr32 = fview(QR32_O, 512)
                act(lambda e, o=qr32, i=bank[:, :]: e.copy(o, i))
                tev = act(lambda e, o=qr[pc], i=bank[:, :]: e.copy(o, i), fast=True)
                nop("tensor", waits=[("act", tev)])
                tpm = mm(pb[5][:, :], sb["PSWPb"][:, :], qr[pc], True, True,
                         waits=[("dve", state.get(("pb5_free",), 0))])
                tma, tmb = fview(TMA_O, 512), fview(TMB_O, 512)
                t1 = dve(lambda e, o=tma, a=pb[5][:, :], b=fview(SGQ_O, 512): e.tensor_mul(o, a, b),
                         waits=[("pe", tpm), ("tab", tok_sgnq)])
                state[("pb5_free",)] = t1
                dve(lambda e, o=tmb, a=qr32, b=fview(CSQ_O, 512): e.tensor_mul(o, a, b),
                    waits=[("tab", tok_cosq)])
                trr = dve(lambda e, o=qr[pc], a=tma, b=tmb: e.tensor_add(o, a, b))
                state[("qrope_done", oc)] = trr
            state[("qevac", oc)] = tev
            tok_normq[oc] = mm(pb[4][0:16, :], sb["EHQb"][:, oc * 16:(oc + 1) * 16],
                               sqst[oc % 2], oc == 0, oc == 23, waits=[("act", tsq)])
        if "a" in SUBJ:
            P.junction()
        tmq = dve(lambda e, o=stats[0:16, 0:1], i=pb[4][0:16, :]: e.reduce_max(o, i, AX.X),
                  waits=[("pe", tok_normq[23])])

        tok_kn_st = None
        tok_normk = {}
        tok_kb = {}
        for oc in range(16):
            g, r = oc // 4, oc % 4
            tok_kb[oc] = DMA("sync", f"kbb{oc % 2}", bview(KBB_O + (oc % 2) * 512, 512),
                             d["s_kbn"][g][:, r * 512:(r + 1) * 512],
                             waits=[("cvo", mark["kb"]),
                                    ("pe", state.get(("kb_free", qt, oc - 2), 0))])
            if oc == 0:
                nop("tensor", waits=[("act", tok_ckvnT[(3, 3)])])
            bank = pb[oc % 2]
            w = [(f"kbb{oc % 2}", tok_kb[oc]), ("act", state.get(("knevac", oc - 2), 0)),
                 ("act", tok_ckvnT[(3, 3)])]
            for cc in range(4):
                tk = mm(bank[:, :], bview(KBB_O + (oc % 2) * 512, 512)[:, cc * 128:(cc + 1) * 128],
                        bview(CKT_O + cc * 512, 512), cc == 0, cc == 3, waits=(w if cc == 0 else ()),
                        inc="pe" if cc == 3 else None)
            state[("kb_free", qt, oc)] = tk
            tsq = act(lambda e, o=sqst[oc % 2], i=bank[:, :]: e.activation(o, i, AF.Square),
                      waits=[("pe", tk), ("pe", tok_normk.get(oc - 2, 0)),
                             ("dve", tmq if oc == 0 else 0)])
            tev = act(lambda e, o=bview(KNST_O + (oc % 2) * 512, 512), i=bank[:, :]: e.copy(o, i), fast=True,
                      waits=[(f"ks{oc % 2}", state.get(("knst_st", oc - 2), 0))])
            state[("knevac", oc)] = tev
            tok_kn_st = DMA("scalar", f"ks{oc % 2}", d["s_kn"][oc][:, T0:kend],
                            bview(KNST_O + (oc % 2) * 512, 512))
            state[("knst_st", oc)] = tok_kn_st
            tok_normk[oc] = mm(pb[4][0:16, :], sb["EHQb"][:, oc * 16:(oc + 1) * 16],
                               sqst[oc % 2], oc == 0, False, waits=[("act", tsq)])
        tkq = act(lambda e, o=bview(KRST_O, 512)[0:64, :], i=krstage[0:64, :]:
                  e.activation(o, i, AF.Square), waits=[("pe", tok_normk[15])])
        tks = mm(pb[5][0:1, :], sb["o64"][0:64, 0:1], bview(KRST_O, 512)[0:64, :], True, True,
                 waits=[("act", tkq), ("dve", state.get(("pb5_free",), 0))])
        tkr2 = act(lambda e, o=sb["krow"][0:1, :], i=pb[5][0:1, :]: e.copy(o, i), waits=[("pe", tks)])
        tbc = mm(pb[4][0:16, :], sb["o116"][0:1, 0:16], sb["krow"][0:1, :], False, True,
                 waits=[("act", tkr2)])
        t1 = dve(lambda e, o=stats[0:16, 1:2], i=pb[4][0:16, :]: e.reduce_max(o, i, AX.X),
                 waits=[("pe", tbc)])
        tmk = dve(lambda e, o=stats[0:16, 2:3], a=stats[0:16, 2:3], b=stats[0:16, 1:2]:
                  e.tensor_max(o, a, b))
        nop("scalar", waits=[("dve", tmk)])
        act(lambda e, o=stats[0:16, 3:4], i=stats[0:16, 0:1]: e.activation(o, i, AF.Ln))
        tl2 = act(lambda e, o=stats[0:16, 4:5], i=stats[0:16, 2:3]: e.activation(o, i, AF.Ln))
        nop("vector", waits=[("act", tl2)])
        tls = dve(lambda e, o=stats[0:16, 5:6], a=stats[0:16, 3:4], b=stats[0:16, 4:5]:
                  e.tensor_add(o, a, b))
        nop("scalar", waits=[("dve", tls)])
        tmh = act(lambda e, o=stats[0:16, 6:7], i=stats[0:16, 5:6]:
                  e.activation(o, i, AF.Exp, scale=sb["cst"][0:16, 4:5], bias=sb["cst"][0:16, 5:6]))
        tnm = dve(lambda e, o=stats[0:16, 7:8], i=stats[0:16, 6:7]: e.tensor_scalar_mul(o, i, -1.0),
                  waits=[("act", tmh)])
        nop("scalar", waits=[("dve", tnm)])
        tnb = act(lambda e, o=sb["nmhb"][0:16, 0:1], i=stats[0:16, 7:8]: e.copy(o, i))
        nop("tensor", waits=[("act", tnb)])
        ttr = E("tensor", lambda e: e.transpose(state["pbT"][1][0:1, 0:16], sb["nmhb"][0:16, 0:1],
                                                sb["IDTb"][0:16, 0:16]),
                waits=[("act", state.get(("pbT_free", 1), 0))], inc="pe")
        trw = act(lambda e, o=sb["nmrow"][0:1, 0:16], i=state["pbT"][1][0:1, 0:16]: e.copy(o, i),
                  waits=[("pe", ttr)])
        state[("pbT_free", 1)] = trw
        nop("tensor", waits=[("act", trw)])
        tbm = mm(pb[5][:, 0:16], sb["o1128"][0:1, :], sb["nmrow"][0:1, 0:16], True, True)
        tok_negm = act(lambda e, o=sb["negm"][:, :], i=pb[5][:, 0:16]: e.copy(o, i),
                       waits=[("pe", tbm)])

        if "a" in SUBJ:
            P.junction()
        tok_v_st = None
        tok_vsl = {}
        for vg in range(4):
            tok_vsl[vg] = DMA("sync", f"vs{vg % 2}",
                              bview(VSL_O + (vg % 2) * 2048, 2048).rearrange("p (a b) -> p a b", b=512),
                              d["s_kbv"].transpose([1, 0, 2])[:, :, vg * 512:(vg + 1) * 512],
                              waits=[("cvo", mark["kb"]),
                                     ("pe", state.get(("vsl_free", qt, vg - 2), 0))])
            tv = None
            for j in range(4):
                bank = pb[2 + (j % 2)]
                w = [(f"vs{vg % 2}", tok_vsl[vg]), ("act", state.get(("vbank_free", j % 2), 0)),
                     ("dve", state[("ckv_free", qt, 3)]),
                     ("act", tok_ckvnT[(3, 3)])]
                for cc in range(4):
                    tv = mm(bank[:, :], bview(CKT_O + cc * 512, 512)[:, j * 128:(j + 1) * 128],
                            bview(VSL_O + (vg % 2) * 2048, 2048)[:, cc * 512:(cc + 1) * 512],
                            cc == 0, cc == 3, waits=(w if cc == 0 else ()),
                            inc="pe" if cc == 3 else None)
                vst = bview(VST_O + (j % 2) * 512, 512)
                tve = act(lambda e, o=vst, i=bank[:, :]: e.copy(o, i), fast=True,
                          waits=[("pe", tv), (f"vt{j % 2}", state.get(("vst_st", j % 2), 0))])
                state[("vbank_free", j % 2)] = tve
                for hx in range(4):
                    hh = vg * 4 + hx
                    tok_v_st = DMA("scalar", f"vt{j % 2}", d["s_v"][hh, 4 * qt + j],
                                   vst[:, hx * 128:(hx + 1) * 128], waits=[("act", tve)])
                state[("vst_st", j % 2)] = tok_v_st
            state[("vsl_free", qt, vg)] = tv

        P.junction()

        krb = bview(KRB_O, 4096)
        tok_krb = DMA("sync", "krb", krb[:, 0:kend], d["s_kr"][:, 0:kend],
                      waits=[("ks0", P.cnt["ks0"]), ("ks1", P.cnt["ks1"]),
                             ("vt0", P.cnt["vt0"]), ("vt1", P.cnt["vt1"]),
                             ("krs", P.cnt["krs"])])
        tok_vones = None
        for p_ in range(2):
            vb_ones = bview(VB_O + p_ * 4160, 4160).rearrange("p (k c) -> p k c", c=130)[:, :, 128:130]
            tok_vones = dve(lambda e, a=vb_ones: e.memset(a, 1.0))
        pvdone = {}
        scdone = {}
        sb_rot = [0, 1, 4]
        sbank_free = {0: 0, 1: 0, 2: 0}
        pt_free = {0: 0, 1: 0, 2: 0}
        for hh in range(16):
            p = hh % 2
            tok_knb = DMA("sync", f"kn{p}", bview(KNB_O + p * 4096, 4096)[:, 0:kend],
                          d["s_kn"][hh][:, 0:kend],
                          waits=[("pe", scdone.get(hh - 2, 0))])
            tok_vb = DMA("sync", f"vb{p}",
                         bview(VB_O + p * 4160, 4160).rearrange("p (k c) -> p k c", c=130)[:, 0:nkc, 0:128],
                         d["s_v"][hh].transpose([1, 0, 2])[:, 0:nkc, :],
                         waits=[("pe", pvdone.get(hh - 2, 0))])
            knb = bview(KNB_O + p * 4096, 4096)
            vbv = bview(VB_O + p * 4160, 4160)
            qrv = bview(QR_O + (hh // 2) * 512, 512)
            r0 = (hh % 2) * 64
            tok_exp, tok_tri = {}, {}

            def scores(kc):
                qoff = max(0, (kc - 4 * qt) * 128)
                bank = pb[sb_rot[kc % 3]]
                w = [(f"kn{p}", tok_knb), ("krb", tok_krb),
                     ("act", sbank_free[kc % 3])]
                mm(bank[:, 0:TT - qoff], knb[:, kc * 128:(kc + 1) * 128],
                   qn[hh][:, qoff:TT], True, False, waits=w, inc=None)
                return mm(bank[:, 0:TT - qoff], krb[r0:r0 + 64, kc * 128:(kc + 1) * 128],
                          qrv[r0:r0 + 64, qoff:TT], False, True)

            def expmask(kc, tsc):
                qoff = max(0, (kc - 4 * qt) * 128)
                bank = pb[sb_rot[kc % 3]]
                pt = bview(PT_O + (kc % 3) * 512, 512)
                te = act(lambda e, o=pt[:, 0:TT - qoff], i=bank[:, 0:TT - qoff]:
                         e.activation(o, i, AF.Exp, scale=sb["cst"][:, 6:7], bias=sb["negm"][:, hh:hh + 1]),
                         waits=[("pe", tsc), ("pe", pt_free[kc % 3])], fast=True)
                tok_exp[kc] = te
                sbank_free[kc % 3] = te
                if kc >= 4 * qt:
                    tok_tri[kc] = dve(lambda e, o=pt[:, 0:128], a=pt[:, 0:128]:
                                      e.tensor_mul(o, a, sb["TRIb"][:, :]), waits=[("act", te)], fast=True)

            def pv_one(kc, pq):
                qoff = max(0, (kc - 4 * qt) * 128)
                p0 = max(0, kc - 4 * qt)
                pt = bview(PT_O + (kc % 3) * 512, 512)
                col = pq * 128 - qoff
                bank = pb[2 + pq // 2]
                diag = (pq == p0 and kc >= 4 * qt)
                if diag and kc == 0:
                    nop("tensor", waits=[("dve", tok_tri[kc])])
                w = [(f"vb{p}", tok_vb), ("dve", tok_vones),
                     ("dve", tok_tri.get(kc, 0) if diag else 0),
                     ("act", tok_exp[kc]),
                     ("act", state.get(("pO_free", pq // 2), 0) if kc == 0 else 0)]
                start = (kc == 0 and pq % 2 == 0)
                stop = (pq % 2 == 1 and kc == 4 * qt + pq)
                t = mm(bank[:, (pq % 2) * 129:(pq % 2) * 129 + 129],
                       pt[:, col:col + 128], vbv[:, kc * 130:kc * 130 + 129],
                       start, stop, waits=w)
                pt_free[kc % 3] = t
                return t

            def pv(kc, deferred):
                p0 = max(0, kc - 4 * qt)
                tlast = None
                for dq in deferred:
                    tlast = pv_one(*dq)
                deferred.clear()
                if kc >= 4 * qt and kc > 0:
                    deferred.append((kc, p0))
                    lo = p0 + 1
                else:
                    lo = p0
                for pq in range(lo, 4):
                    tlast = pv_one(kc, pq)
                return tlast

            if "b" in SUBJ:
                P.junction()
            nk = nkc
            deferred = []
            tpv_last = None
            ts_prev = scores(0)
            for kc in range(1, nk):
                ts = scores(kc)
                expmask(kc - 1, ts_prev)
                t = pv(kc - 1, deferred)
                tpv_last = t if t is not None else tpv_last
                ts_prev = ts
            expmask(nk - 1, ts_prev)
            t = pv(nk - 1, deferred)
            tpv_last = t if t is not None else tpv_last
            for dq in deferred:
                tpv_last = pv_one(*dq)
            deferred.clear()
            scdone[hh] = ts_prev
            pvdone[hh] = tpv_last

            toknorm = {}
            for pq in range(4):
                bank = pb[2 + pq // 2]
                ell = bank[:, (pq % 2) * 129 + 128:(pq % 2) * 129 + 129]
                ov = bank[:, (pq % 2) * 129:(pq % 2) * 129 + 128]
                act(lambda e, o=sb["rl"][:, 0:1], i=ell: e.activation(o, i, AF.Ln),
                    waits=[("pe", tpv_last)])
                trc = act(lambda e, o=sb["rl"][:, 1:2], i=sb["rl"][:, 0:1]:
                          e.activation(o, i, AF.Exp, scale=sb["cst"][:, 7:8]))
                ost = bview(OST_O + pq * 128, 128)
                tn = act(lambda e, o=ost, i=ov, s=sb["rl"][:, 1:2]:
                         e.activation(o, i, AF.Copy, scale=s),
                         waits=[("pe", state.get(("ost_free", pq), 0))], fast=True)
                toknorm[pq] = tn
                state[("pO_free", pq // 2)] = tn
            for pq in range(4):
                tp = transpose(state["pbT"][pq % 2][:, 0:128], bview(OST_O + pq * 128, 128),
                               waits=[("act", toknorm[pq]), ("act", state.get(("pbT_free", pq % 2), 0))])
                te = act(lambda e, o=OTv[hh][:, pq * 128:(pq + 1) * 128],
                         i=state["pbT"][pq % 2][:, 0:128]: e.copy(o, i), waits=[("pe", tp)], fast=True)
                state[("pbT_free", pq % 2)] = te
                state[("ost_free", pq)] = tp

        P.junction()

        for g in reversed(state["pbT_g"]):
            g.__exit__(None, None, None)
        pb67_g = [nc.psum_tensor(f"pb6_{qt}", [128, 512], F32),
                  nc.psum_tensor(f"pb7_{qt}", [128, 512], F32)]
        pb67 = [g.__enter__() for g in pb67_g]
        pbR = pb + pb67

        if qt + 1 < NT:
            for j in range(2):
                state[("tok_x_pre", qt + 1, j)] = DMA(
                    "sync", f"xb{j}", fview(XB_O + j * 2048, 2048),
                    d["x"][(qt + 1) * TT + j * 128:(qt + 1) * TT + (j + 1) * 128, :],
                    waits=[("dve", state.get(("xbfree", j), 0))])

        hb = [bview(HBUF_O + ic * 512, 512) for ic in range(64)]
        tok_h = {}
        for ic in range(64):
            p = ic % 2
            tg = DMA("sync", f"gb{p}", bview(GB_O + p * 2048, 2048), d["s_g"][ic],
                     waits=[("cvo", mark[("g", ic)]), ("pe", state.get(("gb_free", qt, ic - 2), 0))])
            tu = DMA("sync", f"ub{p}", bview(UB_O + p * 2048, 2048), d["s_u"][ic],
                     waits=[("cvo", mark[("u", ic)]), ("pe", state.get(("ub_free", qt, ic - 2), 0))])
            gbank, ubank = pb[p], pb[2 + p]
            w = [(f"gb{p}", tg), ("act", state.get(("sg_done", ic - 2), 0))]
            for dc in range(16):
                tgm = mm(gbank[:, :], bview(GB_O + p * 2048, 2048)[:, dc * 128:(dc + 1) * 128],
                         xnT[dc], dc == 0, dc == 15, waits=(w if dc == 0 else ()),
                         inc="pe" if dc == 15 else None)
            state[("gb_free", qt, ic)] = tgm
            w = [(f"ub{p}", tu), ("act", state.get(("us_done", ic - 2), 0))]
            for dc in range(16):
                tum = mm(ubank[:, :], bview(UB_O + p * 2048, 2048)[:, dc * 128:(dc + 1) * 128],
                         xnT[dc], dc == 0, dc == 15, waits=(w if dc == 0 else ()),
                         inc="pe" if dc == 15 else None)
            state[("ub_free", qt, ic)] = tum
            sg = bview(SGST_O, 512)
            us = bview(UST_O, 512)
            tsg = act(lambda e, o=sg, i=gbank[:, :], s=sb["gscb"][:, ic:ic + 1]:
                      e.activation(o, i, AF.Silu, scale=s),
                      waits=[("pe", tgm), ("dve", tok_h.get(ic - 1, 0))], fast=True)
            state[("sg_done", ic)] = tsg
            tus = act(lambda e, o=us, i=ubank[:, :], s=sb["uscb"][:, ic:ic + 1]:
                      e.activation(o, i, AF.Copy, scale=s), waits=[("pe", tum)], fast=True)
            state[("us_done", ic)] = tus
            tok_h[ic] = dve(lambda e, o=hb[ic], a=sg, b=us: e.tensor_mul(o, a, b),
                            waits=[("act", tus)], fast=True)

        if "c" in SUBJ:
            P.junction()
        for egh in range(2):
            tok_R = None
            for ic in range(64):
                p = ic % 3
                wprev = (state.get(("wdb_free", qt, egh, ic - 3), 0) if ic >= 3
                         else state.get(("wdb_free", qt, egh - 1, 61 + ic), 0))
                tw = DMA("sync", f"wd{p}", bview(WDB_O + p * 1024, 1024), d["s_wd"][egh, ic],
                         waits=[("cvo", mark[("wd", ic)]), ("pe", wprev),
                                ("dve", state.get(("evac_done", qt, egh - 1), 0) if ic == 0 else 0)])
                w0 = [(f"wd{p}", tw), ("dve", tok_h[63] if ic == 0 else 0)]
                for j in range(4):
                    for eg in range(2):
                        tok_R = mm(pbR[j * 2 + eg][:, :], hb[ic][:, j * 128:(j + 1) * 128],
                                   bview(WDB_O + p * 1024, 1024)[:, eg * 512:(eg + 1) * 512],
                                   ic == 0, False,
                                   waits=(w0 if (j == 0 and eg == 0) else ()),
                                   inc="pe" if (j == 3 and eg == 1) else None)
                state[("wdb_free", qt, egh, ic)] = tok_R
            for hh in range(16):
                p = hh % 2
                wprev = (state.get(("wob_free", qt, egh, hh - 2), 0) if hh >= 2
                         else state.get(("wob_free", qt, egh - 1, 14 + hh), 0))
                tw = DMA("sync", f"wo{p}", bview(WOB_O + p * 1024, 1024), d["s_wo"][egh, hh],
                         waits=[("cvo", mark[("wo", hh)]), ("pe", wprev)])
                w0 = [(f"wo{p}", tw)]
                for j in range(4):
                    for eg in range(2):
                        tok_R = mm(pbR[j * 2 + eg][:, :], OTv[hh][:, j * 128:(j + 1) * 128],
                                   bview(WOB_O + p * 1024, 1024)[:, eg * 512:(eg + 1) * 512],
                                   False, hh == 15, waits=(w0 if (j == 0 and eg == 0) else ()),
                                   inc="pe" if (j == 3 and eg == 1) else None)
                state[("wob_free", qt, egh, hh)] = tok_R
            tev = None
            for j in range(4):
                p = j % 2
                xprev = (state.get(("xr_free", qt, egh, j - 2), 0) if j >= 2
                         else state.get(("xr_free", qt, egh - 1, 2 + j), 0))
                txr = DMA("sync", f"xr{p}", fview(XR_O + p * 1024, 1024),
                          d["x"][T0 + j * 128:T0 + (j + 1) * 128, egh * 1024:(egh + 1) * 1024],
                          waits=[("dve", xprev)])
                outs = fview(OUTS_O + j * 1024, 1024)
                for eg in range(2):
                    tev = dve(lambda e, o=outs[:, eg * 512:(eg + 1) * 512],
                              a=pbR[j * 2 + eg][:, :],
                              b=fview(XR_O + p * 1024, 1024)[:, eg * 512:(eg + 1) * 512]:
                              e.tensor_add(o, a, b), fast=True,
                              waits=[("pe", tok_R), (f"xr{p}", txr),
                                     (f"ot{j}", state.get(("out_prev", j), 0))])
                state[("xr_free", qt, egh, j)] = tev
                state[("out_prev", j)] = DMA(
                    "scalar", f"ot{j}", d["out"][T0 + j * 128:T0 + (j + 1) * 128, egh * 1024:(egh + 1) * 1024],
                    outs, waits=[("dve", tev)])
            state[("evac_done", qt, egh)] = tev
            if "c" in SUBJ:
                P.junction()

        state[("evac_prev",)] = tev
        state[("tokR_prev",)] = tok_R
        state[("ot_prev",)] = tuple(P.cnt[f"ot{j}"] for j in range(4))

        for g in reversed(pb67_g):
            g.__exit__(None, None, None)
        state["pbT_g"] = [nc.psum_tensor(f"pbT0_{qt + 1}", [128, 1024], BF16),
                          nc.psum_tensor(f"pbT1_{qt + 1}", [128, 1024], BF16)]
        state["pbT"] = [g.__enter__() for g in state["pbT_g"]]
        for k in (("pbT_free", 0), ("pbT_free", 1)):
            state[k] = 0

    for j in range(4):
        fin = P.cnt[f"ot{j}"]
        E("scalar", lambda e, s=f"ot{j}", v=fin: e.wait_ge(sems[s], v))
    for g in reversed(state["pbT_g"]):
        g.__exit__(None, None, None)

    with nc.Block() as blk:
        def runner(eng_name):
            drainy = eng_name in ("vector", "scalar", "gpsimd")
            own = OWNSEM.get(eng_name)

            def run(eng):
                c_own = 0
                for fn, ws, incs, is_dma in P.q[eng_name]:
                    own_inc = 0
                    if not is_dma:
                        for s, n in incs:
                            if s == own:
                                own_inc += n
                    for s, v in ws:
                        if s == own and v <= c_own and drainy:
                            continue
                        eng.wait_ge(sems[s], v)
                    inst = fn(eng)
                    if is_dma or not drainy:
                        for s, n in incs:
                            inst = inst.then_inc(sems[s], n)
                    else:
                        for s, n in incs:
                            eng.drain().then_inc(sems[s], n)
                    c_own += own_inc
            return run
        blk.sync(runner("sync"))
        blk.tensor(runner("tensor"))
        blk.scalar(runner("scalar"))
        blk.vector(runner("vector"))
        blk.gpsimd(runner("gpsimd"))

    stack.close()
    return nc


_CACHE = {}


def kernel(**inputs):
    x = np.asarray(inputs["L0_attn_x"])
    hw = _host_weights(inputs)
    ht = _host_tables()
    shared = {}
    shared.update(hw)
    shared.update(ht)
    nc = build_program()
    in_maps = []
    for b in range(8):
        m = dict(shared)
        m["x"] = np.ascontiguousarray(x[b])
        in_maps.append(m)
    res = run_bass_kernel_spmd(nc, in_maps, core_ids=list(range(8)))
    out = np.stack([res.results[b]["out"] for b in range(8)], axis=0)
    return out


# version 12 (final)
# speedup vs baseline: 1.1994x; 1.1994x over previous
import math
import numpy as np
import ml_dtypes
from contextlib import ExitStack

import concourse.bass as bass
import concourse.mybir as mybir
from concourse.bass_utils import run_bass_kernel_spmd

F32 = mybir.dt.float32
BF16 = mybir.dt.bfloat16
I32 = mybir.dt.int32
AF = mybir.ActivationFunctionType
OP = mybir.AluOpType
AX = mybir.AxisListType
BF = ml_dtypes.bfloat16

S, D, NH, DN, DR, DV, C, I = 4096, 2048, 16, 128, 64, 128, 512, 8192
NT, TT = 8, 512
SCALE = 1.0 / math.sqrt(DN + DR)
LN_EPS = 1e-5
RMS_EPS = 1.1920928955078125e-07
MHAT_BIAS = math.log(SCALE) + 0.25 - 18.0

import os as _os
SUBJ = _os.environ.get("SUBJ", "")
ENGS = ("sync", "tensor", "scalar", "vector", "gpsimd")
OWNSEM = {"tensor": "pe", "scalar": "act", "vector": "dve", "gpsimd": "pool"}
JSEM = {"tensor": "jpe", "scalar": "jact", "vector": "jdve"}


def _host_tables():
    t = {}
    f = (10000.0 ** (-np.arange(0, DR, 2, dtype=np.float64) / DR))
    ang = np.arange(S, dtype=np.float64)[:, None] * f[None, :]
    cos, sin = np.cos(ang), np.sin(ang)
    cosq = np.empty((128, S), np.float32)
    sgnq = np.empty((128, S), np.float32)
    for r in range(128):
        j = r % 32
        cosq[r] = cos[:, j]
        sgnq[r] = (-sin[:, j]) if (r % 64) < 32 else sin[:, j]
    t["COSQ"] = cosq
    t["SGNQ"] = sgnq
    kt = np.empty((S, 64), np.float32)
    kt[:, 0:32] = cos
    kt[:, 32:64] = sin
    t["KTAB"] = kt
    t["IDT"] = np.eye(128, dtype=BF)
    t["TRI"] = np.triu(np.ones((128, 128), np.float32)).astype(BF)
    pswp = np.zeros((128, 128), np.float32)
    for r in range(128):
        pswp[r, r ^ 32] = 1.0
    t["PSWP"] = pswp.astype(BF)
    ehq = np.zeros((128, 24 * 16), np.float32)
    for oc in range(16):
        ehq[:, oc * 16 + oc] = 1.0
    for pc in range(8):
        oc = 16 + pc
        ehq[0:64, oc * 16 + 2 * pc] = 1.0
        ehq[64:128, oc * 16 + 2 * pc + 1] = 1.0
    t["EHQ"] = ehq.astype(BF)
    t["ONES64"] = np.ones((64, 1), np.float32).astype(BF)
    t["ONES1_16"] = np.ones((1, 16), np.float32).astype(BF)
    t["ONES1_128"] = np.ones((1, 128), np.float32).astype(BF)
    cst = np.zeros((128, 8), np.float32)
    cst[:, 0] = LN_EPS
    cst[:, 1] = -0.5
    cst[:, 2] = 1.0 / 512
    cst[:, 3] = RMS_EPS
    cst[:, 4] = 0.5
    cst[:, 5] = MHAT_BIAS
    cst[:, 6] = SCALE
    cst[:, 7] = -1.0
    t["CONSTS"] = cst
    return t


def _host_weights(inp):
    h = {}
    Wq = np.asarray(inp["L0_attn_Wq"])
    perm = []
    for hh in range(NH):
        perm += [hh * 192 + j for j in range(128)]
    for pc in range(8):
        for hh in (2 * pc, 2 * pc + 1):
            perm += [hh * 192 + 128 + 2 * j for j in range(32)]
            perm += [hh * 192 + 128 + 2 * j + 1 for j in range(32)]
    A = Wq[:, perm].reshape(16, 128, 24, 128)
    h["h_wq"] = np.ascontiguousarray(A.transpose(2, 1, 0, 3).reshape(24, 128, 2048))

    Wkva = np.asarray(inp["L0_attn_Wkv_a"])
    kperm = list(range(512)) + [512 + 2 * j for j in range(32)] + [512 + 2 * j + 1 for j in range(32)]
    B = Wkva[:, kperm].reshape(16, 128, 576)
    h["h_wkva"] = np.ascontiguousarray(B.transpose(1, 0, 2).reshape(128, 16 * 576))

    Wkvb = np.asarray(inp["L0_attn_Wkv_b"])
    kn_cols = [hh * 256 + j for hh in range(NH) for j in range(128)]
    v_cols = [hh * 256 + 128 + j for hh in range(NH) for j in range(128)]
    Bk = Wkvb[:, kn_cols].reshape(4, 128, 16, 128).transpose(2, 1, 0, 3)
    h["h_kbn"] = np.ascontiguousarray(Bk.reshape(16, 128, 512).reshape(4, 4, 128, 512)
                                      .transpose(0, 2, 1, 3).reshape(4, 128, 2048))
    h["h_kbv"] = np.ascontiguousarray(Wkvb[:, v_cols].reshape(4, 128, 2048))

    Wo = np.asarray(inp["L0_attn_Wo"])
    h["h_wo"] = np.ascontiguousarray(Wo.reshape(16, 128, 2048))

    gw = np.asarray(inp["L0_ffn_gate_w"])
    uw = np.asarray(inp["L0_ffn_up_w"])
    dw = np.asarray(inp["L0_ffn_down_w"])
    G = gw.reshape(64, 128, 16, 128)
    h["h_g"] = np.ascontiguousarray(
        G.transpose(0, 3, 2, 1).reshape(64, 128, 2048).view(np.uint8)[..., 0::4])
    U = uw.reshape(64, 128, 16, 128)
    h["h_u"] = np.ascontiguousarray(
        U.transpose(0, 3, 2, 1).reshape(64, 128, 2048).view(np.uint8)[..., 0::4])
    Dn = dw.reshape(2048, 64, 128).transpose(1, 2, 0)
    h["h_d"] = np.ascontiguousarray(np.ascontiguousarray(Dn).view(np.uint8)[..., 0::4])

    h["gsc"] = np.ascontiguousarray(np.asarray(inp["L0_ffn_gate_scale"]).reshape(64, 128).T)
    h["usc"] = np.ascontiguousarray(np.asarray(inp["L0_ffn_up_scale"]).reshape(64, 128).T)
    h["dscrep"] = np.ascontiguousarray(np.tile(np.asarray(inp["L0_ffn_down_scale"])[None, :], (128, 1)))
    h["wrep"] = np.ascontiguousarray(np.tile(np.asarray(inp["L0_attn_kv_norm_w"])[None, :], (128, 1)))
    return h


class Prog:
    def __init__(self, nc, stack):
        self.nc = nc
        self.q = {e: [] for e in ENGS}
        self.sems = {}
        self.cnt = {}
        self.hi = {e: {} for e in ENGS}
        self.pending = {}
        names = ["cv", "cvo", "pe", "act", "dve", "pool", "jpe", "jact", "jdve",
                 "tab0", "tab", "wkv", "krb", "krs"]
        for pfx in ("xb", "kt", "wq", "kbb", "vs", "kn", "vb",
                    "gb", "ub", "wd", "wo", "xr", "ks", "vt", "ot"):
            names += [pfx + "0", pfx + "1"]
        names += ["ot2", "ot3", "wd2"]
        for name in names:
            self.sems[name] = stack.enter_context(nc.semaphore(name))
            self.cnt[name] = 0

    def emit(self, eng, fn, waits=(), inc=None, inc_n=1, is_dma=False, fast=False):
        ws = []
        hw = self.hi[eng]
        for s, v in list(self.pending.pop(eng, [])) + list(waits):
            if v is None or v <= 0:
                continue
            if hw.get(s, -1) >= v:
                continue
            hw[s] = v
            ws.append((s, v))
        tok = None
        incs = []
        if inc is not None:
            self.cnt[inc] += inc_n
            tok = self.cnt[inc]
            incs.append((inc, inc_n))
        self.q[eng].append([fn, ws, incs, is_dma or fast])
        return tok

    def dma(self, eng, sem, out_ap, in_ap, waits=(), **kw):
        fn = (lambda e, o=out_ap, i=in_ap, kw=kw: e.dma_start(o, i, **kw))
        return self.emit(eng, fn, waits, inc=sem, inc_n=16, is_dma=True)

    def junction(self):
        for eng, js in JSEM.items():
            if self.q[eng]:
                self.emit(eng, lambda e: e.drain(), inc=js, is_dma=True)
        snap = [(s, self.cnt[s]) for s in ("jpe", "jact", "jdve")]
        for eng in ("sync", "tensor", "scalar", "vector"):
            self.pending[eng] = list(snap)


def build_program():
    nc = bass.Bass("TRN2", target_bir_lowering=False)
    stack = ExitStack()
    P = Prog(nc, stack)
    d = {}
    sb = {}

    def dram(name, shape, dt, kind="ExternalInput"):
        d[name] = nc.dram_tensor(name, list(shape), dt, kind=kind).ap()
        return d[name]

    def sbuf(name, shape, dt):
        sb[name] = stack.enter_context(nc.sbuf_tensor(name, list(shape), dt))
        return sb[name]

    dram("x", (S, D), F32)
    dram("h_wq", (24, 128, 2048), F32)
    dram("h_wkva", (128, 16 * 576), F32)
    dram("h_kbn", (4, 128, 2048), F32)
    dram("h_kbv", (4, 128, 2048), F32)
    dram("h_wo", (16, 128, 2048), F32)
    U8 = mybir.dt.uint8
    dram("h_g", (64, 128, 2048), U8)
    dram("h_u", (64, 128, 2048), U8)
    dram("h_d", (64, 128, 2048), U8)
    dram("gsc", (128, 64), F32)
    dram("usc", (128, 64), F32)
    dram("dscrep", (128, 2048), F32)
    dram("wrep", (128, 512), F32)
    dram("COSQ", (128, S), F32)
    dram("SGNQ", (128, S), F32)
    dram("KTAB", (S, 64), F32)
    dram("IDT", (128, 128), BF16)
    dram("TRI", (128, 128), BF16)
    dram("PSWP", (128, 128), BF16)
    dram("EHQ", (128, 384), BF16)
    dram("ONES64", (64, 1), BF16)
    dram("ONES1_16", (1, 16), BF16)
    dram("ONES1_128", (1, 128), BF16)
    dram("CONSTS", (128, 8), F32)
    dram("out", (S, D), F32, kind="ExternalOutput")
    dram("s_wq", (24, 128, 2048), BF16, kind="Internal")
    dram("s_wkva", (128, 16 * 576), BF16, kind="Internal")
    dram("s_kbn", (4, 128, 2048), BF16, kind="Internal")
    dram("s_kbv", (4, 128, 2048), BF16, kind="Internal")
    dram("s_wo", (2, 16, 128, 1024), BF16, kind="Internal")
    dram("s_g", (64, 128, 2048), BF16, kind="Internal")
    dram("s_u", (64, 128, 2048), BF16, kind="Internal")
    dram("s_wd", (2, 64, 128, 1024), BF16, kind="Internal")
    dram("s_kn", (16, 128, S), BF16, kind="Internal")
    dram("s_kr", (128, S), BF16, kind="Internal")
    dram("s_v", (16, 32, 128, 128), BF16, kind="Internal")

    sbuf("xnT", (128, 16 * 512), BF16)
    sbuf("OT", (128, 16 * 512), BF16)
    sbuf("cvb", (128, 8192), BF16)
    sbuf("dscb", (128, 2048), BF16)
    sbuf("IDTb", (128, 128), BF16)
    sbuf("TRIb", (128, 128), BF16)
    sbuf("PSWPb", (128, 128), BF16)
    sbuf("EHQb", (128, 384), BF16)
    sbuf("o64", (64, 1), BF16)
    sbuf("o116", (1, 16), BF16)
    sbuf("o1128", (1, 128), BF16)
    sbuf("gscb", (128, 64), F32)
    sbuf("uscb", (128, 64), F32)
    sbuf("wrepb", (128, 512), F32)
    sbuf("negm", (128, 16), F32)
    sbuf("krow", (1, 512), BF16)
    sbuf("st16", (16, 16), F32)
    sbuf("nmhb", (16, 1), BF16)
    sbuf("nmrow", (1, 16), BF16)
    sbuf("rl", (128, 2), F32)
    sbuf("cst", (128, 8), F32)
    sbuf("poolb", (128, 47104), BF16)
    sbuf("poolf", (128, 12304), F32)

    pb = [stack.enter_context(nc.psum_tensor(f"pb{i}", [128, 512], F32)) for i in range(6)]
    pbT_g = [nc.psum_tensor("pbT0_0", [128, 1024], BF16), nc.psum_tensor("pbT1_0", [128, 1024], BF16)]
    pbT = [g.__enter__() for g in pbT_g]

    def bview(off, n):
        return sb["poolb"][:, off:off + n]

    def fview(off, n):
        return sb["poolf"][:, off:off + n]

    QN_O, QR_O, CKT_O = 0, 8192, 12288
    WKVA_O, XNTOK_O, WQB_O, KBB_O = 14336, 23552, 25600, 29696
    SQST_O, KNST_O, KRSQ_O, KRST_O, VST_O, VSL_O = 30720, 31744, 32768, 33280, 33792, 35840
    KRB_O, KNB_O, VB_O, PT_O, OST_O = 14336, 18432, 26624, 34944, 36480
    HBUF_O, GB_O, UB_O, WDB_O, WOB_O, SGST_O, UST_O = 0, 32768, 36864, 40960, 44032, 46080, 46592
    XB_O, XSQ_O, STA_O = 0, 4096, 6144
    CKV_O, T512_O, QR32_O, TMA_O, TMB_O = 6160, 6736, 7248, 7760, 8272
    CSQ_O, SGQ_O, KT_O = 8784, 9296, 9808
    XR_O, OUTS_O = 6160, 8208

    E = P.emit
    DMA = P.dma
    sems = P.sems

    for nm, dst in (("IDT", sb["IDTb"]), ("TRI", sb["TRIb"]), ("PSWP", sb["PSWPb"]),
                    ("EHQ", sb["EHQb"]), ("ONES64", sb["o64"]), ("ONES1_16", sb["o116"]),
                    ("ONES1_128", sb["o1128"]), ("gsc", sb["gscb"]), ("usc", sb["uscb"]),
                    ("wrep", sb["wrepb"]), ("CONSTS", sb["cst"])):
        DMA("sync", "tab0", dst[:, :] if dst.shape[0] == 128 else dst[0:dst.shape[0], :],
            d[nm][0:dst.shape[0], :] if dst.shape[0] != 128 else d[nm][:, :])
    tok_tables = P.cnt["tab0"]
    E("vector", lambda e: e.memset(sb["st16"][0:16, 2:3], 0.0), inc="dve")
    tok_init_dve = P.cnt["dve"]

    cv_half = [sb["cvb"][:, i * 2048:(i + 1) * 2048] for i in range(4)]
    mark = {}

    DMA("gpsimd", "cv", sb["dscb"][:, :], d["dscrep"][:, :])

    units = []
    for k in range(5):
        n = 2048 if k < 4 else 1024
        units.append(("f32", d["h_wkva"][:, k * 2048:k * 2048 + n],
                      [(d["s_wkva"][:, k * 2048:k * 2048 + n], n)], n, "wkva"))
    for oc in range(24):
        units.append(("f32", d["h_wq"][oc], [(d["s_wq"][oc], 2048)], 2048, "wq"))
    for g in range(4):
        units.append(("f32", d["h_kbn"][g], [(d["s_kbn"][g], 2048)], 2048, "kb"))
    for g in range(4):
        units.append(("f32", d["h_kbv"][g], [(d["s_kbv"][g], 2048)], 2048, "kb"))
    for ic in range(64):
        units.append(("int", d["h_g"][ic], [(d["s_g"][ic], 2048)], 2048, ("g", ic)))
        units.append(("int", d["h_u"][ic], [(d["s_u"][ic], 2048)], 2048, ("u", ic)))
    for ic in range(64):
        units.append(("wd", d["h_d"][ic],
                      [(d["s_wd"][0, ic], 1024), (d["s_wd"][1, ic], 1024)], 2048, ("wd", ic)))
    for hh in range(16):
        units.append(("f32", d["h_wo"][hh],
                      [(d["s_wo"][0, hh], 1024), (d["s_wo"][1, hh], 1024)], 2048, ("wo", hh)))

    NU = len(units)
    half_free = {0: None, 1: None, 2: None, 3: None}
    cv_tok = {}
    LOOKAHEAD = 2

    def hw(h):
        f = half_free[h]
        return [f] if f else []

    def conv_prep(i):
        kind, srcap, dsts, n, mk = units[i]
        h = i % 4
        if kind == "f32":
            cv_tok[i] = DMA("gpsimd", "cv", cv_half[h][:, 0:n], srcap, waits=hw(h))
        else:
            tm = E("gpsimd", lambda e, a=cv_half[h][:, 0:n]: e.memset(a, -128.0),
                   waits=hw(h), inc="pool")
            cv_tok[i] = DMA("gpsimd", "cv", cv_half[h][:, 0:n], srcap,
                            waits=[("pool", tm)], accum_op=OP.add)

    def conv_store(i):
        kind, srcap, dsts, n, mk = units[i]
        h = i % 4
        srcbuf = cv_half[h]
        w = [("cv", cv_tok[i])]
        if kind == "wd":
            tt = E("gpsimd", lambda e, o=cv_half[h][:, 0:n], a=cv_half[h][:, 0:n]:
                   e.tensor_mul(o, a, sb["dscb"][:, 0:n]),
                   waits=w, inc="pool")
            w = [("pool", tt)]
        tok = None
        off = 0
        for ds, wd_ in dsts:
            tok = DMA("gpsimd", "cvo", ds, srcbuf[:, off:off + wd_], waits=w)
            off += wd_
        half_free[h] = ("cvo", tok)
        mark[mk] = tok

    for i in range(min(LOOKAHEAD, NU)):
        conv_prep(i)
    for i in range(NU):
        if i + LOOKAHEAD < NU:
            conv_prep(i + LOOKAHEAD)
        conv_store(i)

    cvo_final = P.cnt["cvo"]
    for k in list(mark):
        mark[k] = min(mark[k] + 96, cvo_final)

    state = {"pbT": pbT, "pbT_g": pbT_g}

    def mm(out_ap, lhsT, rhs, start, stop, waits=(), inc="pe"):
        return E("tensor",
                 lambda e, o=out_ap, l=lhsT, r=rhs, s1=start, s2=stop: e.matmul(o, l, r, start=s1, stop=s2),
                 [("tab0", tok_tables)] + list(waits), inc=inc)

    def transpose(out_ap, in_ap, waits=()):
        return E("tensor",
                 lambda e, o=out_ap, i=in_ap: e.transpose(o, i, sb["IDTb"][:, :]),
                 [("tab0", tok_tables)] + list(waits), inc="pe")

    def nop(eng, waits=()):
        return E(eng, lambda e: e.nop(cycle_cnt=384), waits, is_dma=True)

    def act(fn, waits=(), fast=False):
        return E("scalar", fn, [("tab0", tok_tables)] + list(waits), inc="act", fast=fast)

    def dve(fn, waits=(), fast=False):
        return E("vector", fn, [("tab0", tok_tables)] + list(waits), inc="dve", fast=fast)

    xnT = [sb["xnT"][:, c * 512:(c + 1) * 512] for c in range(16)]
    OTv = [sb["OT"][:, hh * 512:(hh + 1) * 512] for hh in range(16)]
    stats = sb["st16"]

    for qt in range(NT):
        T0 = qt * TT
        nkc = 4 * (qt + 1)
        kend = T0 + TT

        ocs = state.get(("ot_prev",), (0, 0, 0, 0))
        DMA("sync", "tab", fview(CSQ_O, 512), d["COSQ"][:, T0:kend],
            waits=[("dve", state.get(("evac_prev",), 0))] +
                  [(f"ot{j}", ocs[j]) for j in range(4)])
        tok_sgnq = tok_cosq = DMA("sync", "tab", fview(SGQ_O, 512), d["SGNQ"][:, T0:kend])
        tok_wkva = DMA("sync", "wkv", bview(WKVA_O, 9216), d["s_wkva"][:, :],
                       waits=[("cvo", mark["wkva"])])

        tok_xnT = {}
        tok_ckvnT = {}
        tok_x, tok_kt = [], []
        last_kr_evac = None
        for j in range(4):
            pre = state.pop(("tok_x_pre", qt, j), None)
            tok_x.append(pre if pre is not None else
                         DMA("sync", f"xb{j % 2}", fview(XB_O + (j % 2) * 2048, 2048),
                             d["x"][T0 + j * 128:T0 + (j + 1) * 128, :],
                             waits=[("dve", state.get(("xbfree", j % 2), 0))]))
            tok_kt.append(DMA("sync", f"kt{j % 2}", fview(KT_O + (j % 2) * 64, 64),
                              d["KTAB"][T0 + j * 128:T0 + (j + 1) * 128, :],
                              waits=[("dve", state.get(("ktab_free", j % 2), 0))]))
            xb = fview(XB_O + (j % 2) * 2048, 2048)
            xsq = fview(XSQ_O, 2048)
            st = fview(STA_O, 16)
            tA = act(lambda e, o=xsq, i=xb: e.activation(o, i, AF.Square),
                     waits=[(f"xb{j % 2}", tok_x[j]), ("dve", state.get(("xsq_free",), 0))])
            t1 = dve(lambda e, o=st[:, 0:1], i=xb: e.reduce_sum(o, i, AX.X), waits=[("act", tA)])
            t2 = dve(lambda e, o=st[:, 1:2], i=xsq: e.reduce_sum(o, i, AX.X))
            state[("xsq_free",)] = t2
            dve(lambda e, o=st[:, 2:3], i=st[:, 0:1]: e.tensor_scalar_mul(o, i, 1.0 / 2048))
            dve(lambda e, o=st[:, 3:4], i=st[:, 1:2]: e.tensor_scalar_mul(o, i, 1.0 / 2048))
            dve(lambda e, o=st[:, 4:5], a=st[:, 2:3]: e.tensor_mul(o, a, a))
            t3 = dve(lambda e, o=st[:, 5:6], a=st[:, 3:4], b=st[:, 4:5]: e.tensor_sub(o, a, b))
            nop("scalar", waits=[("dve", t3)])
            act(lambda e, o=st[:, 6:7], i=st[:, 5:6]: e.activation(o, i, AF.Ln, bias=sb["cst"][:, 0:1]))
            tR = act(lambda e, o=st[:, 7:8], i=st[:, 6:7]: e.activation(o, i, AF.Exp, scale=sb["cst"][:, 1:2]))
            tN = dve(lambda e, o=st[:, 8:9], i=st[:, 2:3]: e.tensor_scalar_mul(o, i, -1.0))
            xntok = bview(XNTOK_O, 2048)
            nop("vector", waits=[("act", tR)])
            tXN = dve(lambda e, o=xntok, i=xb, s1=st[:, 8:9], s2=st[:, 7:8]:
                      e.tensor_scalar(o, i, s1, s2, OP.add, OP.mult),
                      waits=[("pe", state.get(("xntok_free",), 0)),
                             ("pe", state.get(("tokR_prev",), 0))])
            state[("xbfree", j % 2)] = tXN
            for c in range(16):
                tp = transpose(state["pbT"][c % 2][:, 0:128], xntok[:, c * 128:(c + 1) * 128],
                               waits=[("dve", tXN if c == 0 else 0),
                                      ("act", 0 if c == 0 else state.get(("pbT_free", c % 2), 0))])
                if c == 15:
                    state[("xntok_free",)] = tp
                te = act(lambda e, o=xnT[c][:, j * 128:(j + 1) * 128], i=state["pbT"][c % 2][:, 0:128]:
                         e.copy(o, i), waits=[("pe", tp)], fast=True)
                state[("pbT_free", c % 2)] = te
                tok_xnT[(j, c)] = te
            w = [("wkv", tok_wkva), ("act", tok_xnT[(j, 15)]),
                 ("dve", state.get(("ckv_free", qt, j - 1), 0))]
            for dc in range(16):
                wv = w if dc == 0 else ()
                mm(pb[2][:, :], xnT[dc][:, j * 128:(j + 1) * 128],
                   bview(WKVA_O + dc * 576, 512), dc == 0, dc == 15, waits=wv, inc=None)
            for dc in range(16):
                tkva = mm(pb[3][:, 0:64], xnT[dc][:, j * 128:(j + 1) * 128],
                          bview(WKVA_O + dc * 576 + 512, 64), dc == 0, dc == 15,
                          inc="pe" if dc == 15 else None)
            ckv = fview(CKV_O, 576)
            ocs = state.get(("ot_prev",), (0, 0, 0, 0))
            tc1 = dve(lambda e, o=ckv[:, 0:512], i=pb[2][:, :]: e.tensor_copy(o, i),
                      waits=[("pe", tkva)] + [(f"ot{j2}", ocs[j2]) for j2 in range(4)])
            tc2 = dve(lambda e, o=ckv[:, 512:576], i=pb[3][:, 0:64]: e.tensor_copy(o, i))
            state[("ckv_free", qt, j)] = tc2
            kt = fview(KT_O + (j % 2) * 64, 64)
            tma, tmb = fview(TMA_O, 512), fview(TMB_O, 512)
            krt = bview(KRST_O, 512)
            dve(lambda e, o=tma[:, 0:32], a=ckv[:, 512:544], b=kt[:, 0:32]: e.tensor_mul(o, a, b),
                waits=[(f"kt{j % 2}", tok_kt[j])])
            dve(lambda e, o=tmb[:, 0:32], a=ckv[:, 544:576], b=kt[:, 32:64]: e.tensor_mul(o, a, b))
            dve(lambda e, o=krt[:, 0:32], a=tma[:, 0:32], b=tmb[:, 0:32]: e.tensor_sub(o, a, b),
                waits=[("pe", state.get(("krt_free",), 0))])
            dve(lambda e, o=tma[:, 0:32], a=ckv[:, 544:576], b=kt[:, 0:32]: e.tensor_mul(o, a, b))
            tktf = dve(lambda e, o=tmb[:, 0:32], a=ckv[:, 512:544], b=kt[:, 32:64]: e.tensor_mul(o, a, b))
            state[("ktab_free", j % 2)] = tktf
            tkr = dve(lambda e, o=krt[:, 32:64], a=tma[:, 0:32], b=tmb[:, 0:32]: e.tensor_add(o, a, b))
            t512 = fview(T512_O, 512)
            dve(lambda e, o=t512, a=ckv[:, 0:512]: e.tensor_mul(o, a, a))
            tss = dve(lambda e, o=st[:, 9:10], i=t512: e.reduce_sum(o, i, AX.X))
            nop("scalar", waits=[("dve", tss)])
            act(lambda e, o=st[:, 10:11], i=st[:, 9:10]:
                e.activation(o, i, AF.Ln, scale=sb["cst"][:, 2:3], bias=sb["cst"][:, 3:4]))
            tr2 = act(lambda e, o=st[:, 11:12], i=st[:, 10:11]:
                      e.activation(o, i, AF.Exp, scale=sb["cst"][:, 1:2]))
            ckvnb = bview(KNST_O, 512)
            ckvnb = bview(SQST_O, 512)
            nop("vector", waits=[("act", tr2)])
            tck = dve(lambda e, o=ckvnb, a=ckv[:, 0:512], s=st[:, 11:12]:
                      e.scalar_tensor_tensor(o, a, s, sb["wrepb"][:, :], OP.mult, OP.mult),
                      waits=[("pe", state.get(("ckvnb_free",), 0))])
            for cc in range(4):
                tp = transpose(state["pbT"][cc % 2][:, 0:128], ckvnb[:, cc * 128:(cc + 1) * 128],
                               waits=[("dve", tck), ("act", state.get(("pbT_free", cc % 2), 0))])
                te = act(lambda e, o=bview(CKT_O + cc * 512, 512)[:, j * 128:(j + 1) * 128],
                         i=state["pbT"][cc % 2][:, 0:128]: e.copy(o, i), waits=[("pe", tp)], fast=True)
                state[("pbT_free", cc % 2)] = te
                tok_ckvnT[(j, cc)] = te
            state[("ckvnb_free",)] = tp
            nop("tensor", waits=[("dve", tkr)])
            tp = transpose(state["pbT"][0][0:64, 0:128], krt[:, 0:64],
                           waits=[("act", state.get(("pbT_free", 0), 0))])
            state[("krt_free",)] = tp
            te = act(lambda e, o=bview(KRSQ_O, 512)[0:64, j * 128:(j + 1) * 128],
                     i=state["pbT"][0][0:64, 0:128]: e.copy(o, i), waits=[("pe", tp)], fast=True)
            state[("pbT_free", 0)] = te
            last_kr_evac = te

        if "a" in SUBJ:
            P.junction()
        krstage = bview(KRSQ_O, 512)
        DMA("scalar", "krs", d["s_kr"][0:64, T0:kend], krstage[0:64, :],
            waits=[("act", last_kr_evac)])
        tok_kr_st = DMA("scalar", "krs", d["s_kr"][64:128, T0:kend], krstage[0:64, :])

        sqst = [bview(KRST_O, 512), bview(VST_O, 512)]
        qn = [bview(QN_O + oc * 512, 512) for oc in range(16)]
        qr = [bview(QR_O + pc * 512, 512) for pc in range(8)]
        tok_sq = {}
        tok_normq = {}
        tok_wq = {}
        for oc in range(24):
            tok_wq[oc] = DMA("sync", f"wq{oc % 2}", bview(WQB_O + (oc % 2) * 2048, 2048), d["s_wq"][oc],
                             waits=[("cvo", mark["wq"]),
                                    ("pe", state.get(("wq_free", qt, oc - 2), 0))])
            bank = pb[oc % 2]
            w = [(f"wq{oc % 2}", tok_wq[oc]), ("act", state.get(("qevac", oc - 2), 0)),
                 ("dve", state.get(("qrope_done", oc - 2), 0))]
            for dc in range(16):
                tq = mm(bank[:, :], bview(WQB_O + (oc % 2) * 2048, 2048)[:, dc * 128:(dc + 1) * 128],
                        xnT[dc], dc == 0, dc == 15, waits=(w if dc == 0 else ()),
                        inc="pe" if dc == 15 else None)
            state[("wq_free", qt, oc)] = tq
            tsq = act(lambda e, o=sqst[oc % 2], i=bank[:, :]: e.activation(o, i, AF.Square),
                      waits=[("pe", tq), ("pe", tok_normq.get(oc - 2, 0))])
            tok_sq[oc] = tsq
            if oc < 16:
                tev = act(lambda e, o=qn[oc], i=bank[:, :]: e.copy(o, i), fast=True)
            else:
                pc = oc - 16
                qr32 = fview(QR32_O, 512)
                act(lambda e, o=qr32, i=bank[:, :]: e.copy(o, i))
                tev = act(lambda e, o=qr[pc], i=bank[:, :]: e.copy(o, i), fast=True)
                nop("tensor", waits=[("act", tev)])
                tpm = mm(pb[5][:, :], sb["PSWPb"][:, :], qr[pc], True, True,
                         waits=[("dve", state.get(("pb5_free",), 0))])
                tma, tmb = fview(TMA_O, 512), fview(TMB_O, 512)
                t1 = dve(lambda e, o=tma, a=pb[5][:, :], b=fview(SGQ_O, 512): e.tensor_mul(o, a, b),
                         waits=[("pe", tpm), ("tab", tok_sgnq)])
                state[("pb5_free",)] = t1
                dve(lambda e, o=tmb, a=qr32, b=fview(CSQ_O, 512): e.tensor_mul(o, a, b),
                    waits=[("tab", tok_cosq)])
                trr = dve(lambda e, o=qr[pc], a=tma, b=tmb: e.tensor_add(o, a, b))
                state[("qrope_done", oc)] = trr
            state[("qevac", oc)] = tev
            tok_normq[oc] = mm(pb[4][0:16, :], sb["EHQb"][:, oc * 16:(oc + 1) * 16],
                               sqst[oc % 2], oc == 0, oc == 23, waits=[("act", tsq)])
        if "a" in SUBJ:
            P.junction()
        tmq = dve(lambda e, o=stats[0:16, 0:1], i=pb[4][0:16, :]: e.reduce_max(o, i, AX.X),
                  waits=[("pe", tok_normq[23])])

        tok_kn_st = None
        tok_normk = {}
        tok_kb = {}
        for oc in range(16):
            g, r = oc // 4, oc % 4
            tok_kb[oc] = DMA("sync", f"kbb{oc % 2}", bview(KBB_O + (oc % 2) * 512, 512),
                             d["s_kbn"][g][:, r * 512:(r + 1) * 512],
                             waits=[("cvo", mark["kb"]),
                                    ("pe", state.get(("kb_free", qt, oc - 2), 0))])
            if oc == 0:
                nop("tensor", waits=[("act", tok_ckvnT[(3, 3)])])
            bank = pb[oc % 2]
            w = [(f"kbb{oc % 2}", tok_kb[oc]), ("act", state.get(("knevac", oc - 2), 0)),
                 ("act", tok_ckvnT[(3, 3)])]
            for cc in range(4):
                tk = mm(bank[:, :], bview(KBB_O + (oc % 2) * 512, 512)[:, cc * 128:(cc + 1) * 128],
                        bview(CKT_O + cc * 512, 512), cc == 0, cc == 3, waits=(w if cc == 0 else ()),
                        inc="pe" if cc == 3 else None)
            state[("kb_free", qt, oc)] = tk
            tsq = act(lambda e, o=sqst[oc % 2], i=bank[:, :]: e.activation(o, i, AF.Square),
                      waits=[("pe", tk), ("pe", tok_normk.get(oc - 2, 0)),
                             ("dve", tmq if oc == 0 else 0)])
            tev = act(lambda e, o=bview(KNST_O + (oc % 2) * 512, 512), i=bank[:, :]: e.copy(o, i), fast=True,
                      waits=[(f"ks{oc % 2}", state.get(("knst_st", oc - 2), 0))])
            state[("knevac", oc)] = tev
            tok_kn_st = DMA("scalar", f"ks{oc % 2}", d["s_kn"][oc][:, T0:kend],
                            bview(KNST_O + (oc % 2) * 512, 512))
            state[("knst_st", oc)] = tok_kn_st
            tok_normk[oc] = mm(pb[4][0:16, :], sb["EHQb"][:, oc * 16:(oc + 1) * 16],
                               sqst[oc % 2], oc == 0, False, waits=[("act", tsq)])
        tkq = act(lambda e, o=bview(KRST_O, 512)[0:64, :], i=krstage[0:64, :]:
                  e.activation(o, i, AF.Square), waits=[("pe", tok_normk[15])])
        tks = mm(pb[5][0:1, :], sb["o64"][0:64, 0:1], bview(KRST_O, 512)[0:64, :], True, True,
                 waits=[("act", tkq), ("dve", state.get(("pb5_free",), 0))])
        tkr2 = act(lambda e, o=sb["krow"][0:1, :], i=pb[5][0:1, :]: e.copy(o, i), waits=[("pe", tks)])
        tbc = mm(pb[4][0:16, :], sb["o116"][0:1, 0:16], sb["krow"][0:1, :], False, True,
                 waits=[("act", tkr2)])
        t1 = dve(lambda e, o=stats[0:16, 1:2], i=pb[4][0:16, :]: e.reduce_max(o, i, AX.X),
                 waits=[("pe", tbc)])
        tmk = dve(lambda e, o=stats[0:16, 2:3], a=stats[0:16, 2:3], b=stats[0:16, 1:2]:
                  e.tensor_max(o, a, b))
        nop("scalar", waits=[("dve", tmk)])
        act(lambda e, o=stats[0:16, 3:4], i=stats[0:16, 0:1]: e.activation(o, i, AF.Ln))
        tl2 = act(lambda e, o=stats[0:16, 4:5], i=stats[0:16, 2:3]: e.activation(o, i, AF.Ln))
        nop("vector", waits=[("act", tl2)])
        tls = dve(lambda e, o=stats[0:16, 5:6], a=stats[0:16, 3:4], b=stats[0:16, 4:5]:
                  e.tensor_add(o, a, b))
        nop("scalar", waits=[("dve", tls)])
        tmh = act(lambda e, o=stats[0:16, 6:7], i=stats[0:16, 5:6]:
                  e.activation(o, i, AF.Exp, scale=sb["cst"][0:16, 4:5], bias=sb["cst"][0:16, 5:6]))
        tnm = dve(lambda e, o=stats[0:16, 7:8], i=stats[0:16, 6:7]: e.tensor_scalar_mul(o, i, -1.0),
                  waits=[("act", tmh)])
        nop("scalar", waits=[("dve", tnm)])
        tnb = act(lambda e, o=sb["nmhb"][0:16, 0:1], i=stats[0:16, 7:8]: e.copy(o, i))
        nop("tensor", waits=[("act", tnb)])
        ttr = E("tensor", lambda e: e.transpose(state["pbT"][1][0:1, 0:16], sb["nmhb"][0:16, 0:1],
                                                sb["IDTb"][0:16, 0:16]),
                waits=[("act", state.get(("pbT_free", 1), 0))], inc="pe")
        trw = act(lambda e, o=sb["nmrow"][0:1, 0:16], i=state["pbT"][1][0:1, 0:16]: e.copy(o, i),
                  waits=[("pe", ttr)])
        state[("pbT_free", 1)] = trw
        nop("tensor", waits=[("act", trw)])
        tbm = mm(pb[5][:, 0:16], sb["o1128"][0:1, :], sb["nmrow"][0:1, 0:16], True, True)
        tok_negm = act(lambda e, o=sb["negm"][:, :], i=pb[5][:, 0:16]: e.copy(o, i),
                       waits=[("pe", tbm)])

        if "a" in SUBJ:
            P.junction()
        tok_v_st = None
        tok_vsl = {}
        for vg in range(4):
            tok_vsl[vg] = DMA("sync", f"vs{vg % 2}",
                              bview(VSL_O + (vg % 2) * 2048, 2048).rearrange("p (a b) -> p a b", b=512),
                              d["s_kbv"].transpose([1, 0, 2])[:, :, vg * 512:(vg + 1) * 512],
                              waits=[("cvo", mark["kb"]),
                                     ("pe", state.get(("vsl_free", qt, vg - 2), 0))])
            tv = None
            for j in range(4):
                bank = pb[2 + (j % 2)]
                w = [(f"vs{vg % 2}", tok_vsl[vg]), ("act", state.get(("vbank_free", j % 2), 0)),
                     ("dve", state[("ckv_free", qt, 3)]),
                     ("act", tok_ckvnT[(3, 3)])]
                for cc in range(4):
                    tv = mm(bank[:, :], bview(CKT_O + cc * 512, 512)[:, j * 128:(j + 1) * 128],
                            bview(VSL_O + (vg % 2) * 2048, 2048)[:, cc * 512:(cc + 1) * 512],
                            cc == 0, cc == 3, waits=(w if cc == 0 else ()),
                            inc="pe" if cc == 3 else None)
                vst = bview(VST_O + (j % 2) * 512, 512)
                tve = act(lambda e, o=vst, i=bank[:, :]: e.copy(o, i), fast=True,
                          waits=[("pe", tv), (f"vt{j % 2}", state.get(("vst_st", j % 2), 0))])
                state[("vbank_free", j % 2)] = tve
                for hx in range(4):
                    hh = vg * 4 + hx
                    tok_v_st = DMA("scalar", f"vt{j % 2}", d["s_v"][hh, 4 * qt + j],
                                   vst[:, hx * 128:(hx + 1) * 128], waits=[("act", tve)])
                state[("vst_st", j % 2)] = tok_v_st
            state[("vsl_free", qt, vg)] = tv

        P.junction()

        krb = bview(KRB_O, 4096)
        tok_krb = DMA("sync", "krb", krb[:, 0:kend], d["s_kr"][:, 0:kend],
                      waits=[("ks0", P.cnt["ks0"]), ("ks1", P.cnt["ks1"]),
                             ("vt0", P.cnt["vt0"]), ("vt1", P.cnt["vt1"]),
                             ("krs", P.cnt["krs"])])
        tok_vones = None
        for p_ in range(2):
            vb_ones = bview(VB_O + p_ * 4160, 4160).rearrange("p (k c) -> p k c", c=130)[:, :, 128:130]
            tok_vones = dve(lambda e, a=vb_ones: e.memset(a, 1.0))
        pvdone = {}
        scdone = {}
        sb_rot = [0, 1, 4]
        sbank_free = {0: 0, 1: 0, 2: 0}
        pt_free = {0: 0, 1: 0, 2: 0}
        for hh in range(16):
            p = hh % 2
            tok_knb = DMA("sync", f"kn{p}", bview(KNB_O + p * 4096, 4096)[:, 0:kend],
                          d["s_kn"][hh][:, 0:kend],
                          waits=[("pe", scdone.get(hh - 2, 0))])
            tok_vb = DMA("sync", f"vb{p}",
                         bview(VB_O + p * 4160, 4160).rearrange("p (k c) -> p k c", c=130)[:, 0:nkc, 0:128],
                         d["s_v"][hh].transpose([1, 0, 2])[:, 0:nkc, :],
                         waits=[("pe", pvdone.get(hh - 2, 0))])
            knb = bview(KNB_O + p * 4096, 4096)
            vbv = bview(VB_O + p * 4160, 4160)
            qrv = bview(QR_O + (hh // 2) * 512, 512)
            r0 = (hh % 2) * 64
            tok_exp, tok_tri = {}, {}

            def scores(kc):
                qoff = max(0, (kc - 4 * qt) * 128)
                bank = pb[sb_rot[kc % 3]]
                w = [(f"kn{p}", tok_knb), ("krb", tok_krb),
                     ("act", sbank_free[kc % 3])]
                mm(bank[:, 0:TT - qoff], knb[:, kc * 128:(kc + 1) * 128],
                   qn[hh][:, qoff:TT], True, False, waits=w, inc=None)
                return mm(bank[:, 0:TT - qoff], krb[r0:r0 + 64, kc * 128:(kc + 1) * 128],
                          qrv[r0:r0 + 64, qoff:TT], False, True)

            def expmask(kc, tsc):
                qoff = max(0, (kc - 4 * qt) * 128)
                bank = pb[sb_rot[kc % 3]]
                pt = bview(PT_O + (kc % 3) * 512, 512)
                te = act(lambda e, o=pt[:, 0:TT - qoff], i=bank[:, 0:TT - qoff]:
                         e.activation(o, i, AF.Exp, scale=sb["cst"][:, 6:7], bias=sb["negm"][:, hh:hh + 1]),
                         waits=[("pe", tsc), ("pe", pt_free[kc % 3])], fast=True)
                tok_exp[kc] = te
                sbank_free[kc % 3] = te
                if kc >= 4 * qt:
                    tok_tri[kc] = dve(lambda e, o=pt[:, 0:128], a=pt[:, 0:128]:
                                      e.tensor_mul(o, a, sb["TRIb"][:, :]), waits=[("act", te)], fast=True)

            def pv_one(kc, pq):
                qoff = max(0, (kc - 4 * qt) * 128)
                p0 = max(0, kc - 4 * qt)
                pt = bview(PT_O + (kc % 3) * 512, 512)
                col = pq * 128 - qoff
                bank = pb[2 + pq // 2]
                diag = (pq == p0 and kc >= 4 * qt)
                if diag and kc == 0:
                    nop("tensor", waits=[("dve", tok_tri[kc])])
                w = [(f"vb{p}", tok_vb), ("dve", tok_vones),
                     ("dve", tok_tri.get(kc, 0) if diag else 0),
                     ("act", tok_exp[kc]),
                     ("act", state.get(("pO_free", pq // 2), 0) if kc == 0 else 0)]
                start = (kc == 0 and pq % 2 == 0)
                stop = (pq % 2 == 1 and kc == 4 * qt + pq)
                t = mm(bank[:, (pq % 2) * 129:(pq % 2) * 129 + 129],
                       pt[:, col:col + 128], vbv[:, kc * 130:kc * 130 + 129],
                       start, stop, waits=w)
                pt_free[kc % 3] = t
                return t

            def pv(kc, deferred):
                p0 = max(0, kc - 4 * qt)
                tlast = None
                for dq in deferred:
                    tlast = pv_one(*dq)
                deferred.clear()
                if kc >= 4 * qt and kc > 0:
                    deferred.append((kc, p0))
                    lo = p0 + 1
                else:
                    lo = p0
                for pq in range(lo, 4):
                    tlast = pv_one(kc, pq)
                return tlast

            if "b" in SUBJ:
                P.junction()
            nk = nkc
            deferred = []
            tpv_last = None
            ts_prev = scores(0)
            for kc in range(1, nk):
                ts = scores(kc)
                expmask(kc - 1, ts_prev)
                t = pv(kc - 1, deferred)
                tpv_last = t if t is not None else tpv_last
                ts_prev = ts
            expmask(nk - 1, ts_prev)
            t = pv(nk - 1, deferred)
            tpv_last = t if t is not None else tpv_last
            for dq in deferred:
                tpv_last = pv_one(*dq)
            deferred.clear()
            scdone[hh] = ts_prev
            pvdone[hh] = tpv_last

            toknorm = {}
            for pq in range(4):
                bank = pb[2 + pq // 2]
                ell = bank[:, (pq % 2) * 129 + 128:(pq % 2) * 129 + 129]
                ov = bank[:, (pq % 2) * 129:(pq % 2) * 129 + 128]
                act(lambda e, o=sb["rl"][:, 0:1], i=ell: e.activation(o, i, AF.Ln),
                    waits=[("pe", tpv_last)])
                trc = act(lambda e, o=sb["rl"][:, 1:2], i=sb["rl"][:, 0:1]:
                          e.activation(o, i, AF.Exp, scale=sb["cst"][:, 7:8]))
                ost = bview(OST_O + pq * 128, 128)
                tn = act(lambda e, o=ost, i=ov, s=sb["rl"][:, 1:2]:
                         e.activation(o, i, AF.Copy, scale=s),
                         waits=[("pe", state.get(("ost_free", pq), 0))], fast=True)
                toknorm[pq] = tn
                state[("pO_free", pq // 2)] = tn
            for pq in range(4):
                tp = transpose(state["pbT"][pq % 2][:, 0:128], bview(OST_O + pq * 128, 128),
                               waits=[("act", toknorm[pq]), ("act", state.get(("pbT_free", pq % 2), 0))])
                te = act(lambda e, o=OTv[hh][:, pq * 128:(pq + 1) * 128],
                         i=state["pbT"][pq % 2][:, 0:128]: e.copy(o, i), waits=[("pe", tp)], fast=True)
                state[("pbT_free", pq % 2)] = te
                state[("ost_free", pq)] = tp

        P.junction()

        for g in reversed(state["pbT_g"]):
            g.__exit__(None, None, None)
        pb67_g = [nc.psum_tensor(f"pb6_{qt}", [128, 512], F32),
                  nc.psum_tensor(f"pb7_{qt}", [128, 512], F32)]
        pb67 = [g.__enter__() for g in pb67_g]
        pbR = pb + pb67

        if qt + 1 < NT:
            for j in range(2):
                state[("tok_x_pre", qt + 1, j)] = DMA(
                    "sync", f"xb{j}", fview(XB_O + j * 2048, 2048),
                    d["x"][(qt + 1) * TT + j * 128:(qt + 1) * TT + (j + 1) * 128, :],
                    waits=[("dve", state.get(("xbfree", j), 0))])

        hb = [bview(HBUF_O + ic * 512, 512) for ic in range(64)]
        tok_h = {}
        for ic in range(64):
            p = ic % 2
            tg = DMA("sync", f"gb{p}", bview(GB_O + p * 2048, 2048), d["s_g"][ic],
                     waits=[("cvo", mark[("g", ic)]), ("pe", state.get(("gb_free", qt, ic - 2), 0))])
            tu = DMA("sync", f"ub{p}", bview(UB_O + p * 2048, 2048), d["s_u"][ic],
                     waits=[("cvo", mark[("u", ic)]), ("pe", state.get(("ub_free", qt, ic - 2), 0))])
            gbank, ubank = pb[p], pb[2 + p]
            w = [(f"gb{p}", tg), ("act", state.get(("sg_done", ic - 2), 0))]
            for dc in range(16):
                tgm = mm(gbank[:, :], bview(GB_O + p * 2048, 2048)[:, dc * 128:(dc + 1) * 128],
                         xnT[dc], dc == 0, dc == 15, waits=(w if dc == 0 else ()),
                         inc="pe" if dc == 15 else None)
            state[("gb_free", qt, ic)] = tgm
            w = [(f"ub{p}", tu), ("act", state.get(("us_done", ic - 2), 0))]
            for dc in range(16):
                tum = mm(ubank[:, :], bview(UB_O + p * 2048, 2048)[:, dc * 128:(dc + 1) * 128],
                         xnT[dc], dc == 0, dc == 15, waits=(w if dc == 0 else ()),
                         inc="pe" if dc == 15 else None)
            state[("ub_free", qt, ic)] = tum
            sg = bview(SGST_O, 512)
            us = bview(UST_O, 512)
            tsg = act(lambda e, o=sg, i=gbank[:, :], s=sb["gscb"][:, ic:ic + 1]:
                      e.activation(o, i, AF.Silu, scale=s),
                      waits=[("pe", tgm), ("dve", tok_h.get(ic - 1, 0))], fast=True)
            state[("sg_done", ic)] = tsg
            tus = act(lambda e, o=us, i=ubank[:, :], s=sb["uscb"][:, ic:ic + 1]:
                      e.activation(o, i, AF.Copy, scale=s), waits=[("pe", tum)], fast=True)
            state[("us_done", ic)] = tus
            tok_h[ic] = dve(lambda e, o=hb[ic], a=sg, b=us: e.tensor_mul(o, a, b),
                            waits=[("act", tus)], fast=True)

        if "c" in SUBJ:
            P.junction()
        for egh in range(2):
            tok_R = None
            for ic in range(64):
                p = ic % 3
                wprev = (state.get(("wdb_free", qt, egh, ic - 3), 0) if ic >= 3
                         else state.get(("wdb_free", qt, egh - 1, 61 + ic), 0))
                tw = DMA("sync", f"wd{p}", bview(WDB_O + p * 1024, 1024), d["s_wd"][egh, ic],
                         waits=[("cvo", mark[("wd", ic)]), ("pe", wprev),
                                ("dve", state.get(("evac_done", qt, egh - 1), 0) if ic == 0 else 0)])
                w0 = [(f"wd{p}", tw), ("dve", tok_h[63] if ic == 0 else 0)]
                for j in range(4):
                    for eg in range(2):
                        tok_R = mm(pbR[j * 2 + eg][:, :], hb[ic][:, j * 128:(j + 1) * 128],
                                   bview(WDB_O + p * 1024, 1024)[:, eg * 512:(eg + 1) * 512],
                                   ic == 0, False,
                                   waits=(w0 if (j == 0 and eg == 0) else ()),
                                   inc="pe" if (j == 3 and eg == 1) else None)
                state[("wdb_free", qt, egh, ic)] = tok_R
            for hh in range(16):
                p = hh % 2
                wprev = (state.get(("wob_free", qt, egh, hh - 2), 0) if hh >= 2
                         else state.get(("wob_free", qt, egh - 1, 14 + hh), 0))
                tw = DMA("sync", f"wo{p}", bview(WOB_O + p * 1024, 1024), d["s_wo"][egh, hh],
                         waits=[("cvo", mark[("wo", hh)]), ("pe", wprev)])
                w0 = [(f"wo{p}", tw)]
                for j in range(4):
                    for eg in range(2):
                        tok_R = mm(pbR[j * 2 + eg][:, :], OTv[hh][:, j * 128:(j + 1) * 128],
                                   bview(WOB_O + p * 1024, 1024)[:, eg * 512:(eg + 1) * 512],
                                   False, hh == 15, waits=(w0 if (j == 0 and eg == 0) else ()),
                                   inc="pe" if (j == 3 and eg == 1) else None)
                state[("wob_free", qt, egh, hh)] = tok_R
            tev = None
            for j in range(4):
                p = j % 2
                xprev = (state.get(("xr_free", qt, egh, j - 2), 0) if j >= 2
                         else state.get(("xr_free", qt, egh - 1, 2 + j), 0))
                txr = DMA("sync", f"xr{p}", fview(XR_O + p * 1024, 1024),
                          d["x"][T0 + j * 128:T0 + (j + 1) * 128, egh * 1024:(egh + 1) * 1024],
                          waits=[("dve", xprev)])
                outs = fview(OUTS_O + j * 1024, 1024)
                for eg in range(2):
                    tev = dve(lambda e, o=outs[:, eg * 512:(eg + 1) * 512],
                              a=pbR[j * 2 + eg][:, :],
                              b=fview(XR_O + p * 1024, 1024)[:, eg * 512:(eg + 1) * 512]:
                              e.tensor_add(o, a, b), fast=True,
                              waits=[("pe", tok_R), (f"xr{p}", txr),
                                     (f"ot{j}", state.get(("out_prev", j), 0))])
                state[("xr_free", qt, egh, j)] = tev
                state[("out_prev", j)] = DMA(
                    "scalar", f"ot{j}", d["out"][T0 + j * 128:T0 + (j + 1) * 128, egh * 1024:(egh + 1) * 1024],
                    outs, waits=[("dve", tev)])
            state[("evac_done", qt, egh)] = tev
            if "c" in SUBJ:
                P.junction()

        state[("evac_prev",)] = tev
        state[("tokR_prev",)] = tok_R
        state[("ot_prev",)] = tuple(P.cnt[f"ot{j}"] for j in range(4))

        for g in reversed(pb67_g):
            g.__exit__(None, None, None)
        state["pbT_g"] = [nc.psum_tensor(f"pbT0_{qt + 1}", [128, 1024], BF16),
                          nc.psum_tensor(f"pbT1_{qt + 1}", [128, 1024], BF16)]
        state["pbT"] = [g.__enter__() for g in state["pbT_g"]]
        for k in (("pbT_free", 0), ("pbT_free", 1)):
            state[k] = 0

    for j in range(4):
        fin = P.cnt[f"ot{j}"]
        E("scalar", lambda e, s=f"ot{j}", v=fin: e.wait_ge(sems[s], v))
    for g in reversed(state["pbT_g"]):
        g.__exit__(None, None, None)

    with nc.Block() as blk:
        def runner(eng_name):
            drainy = eng_name in ("vector", "scalar", "gpsimd")
            own = OWNSEM.get(eng_name)

            def run(eng):
                c_own = 0
                for fn, ws, incs, is_dma in P.q[eng_name]:
                    own_inc = 0
                    if not is_dma:
                        for s, n in incs:
                            if s == own:
                                own_inc += n
                    for s, v in ws:
                        if s == own and v <= c_own and drainy:
                            continue
                        eng.wait_ge(sems[s], v)
                    inst = fn(eng)
                    if is_dma or not drainy:
                        for s, n in incs:
                            inst = inst.then_inc(sems[s], n)
                    else:
                        for s, n in incs:
                            eng.drain().then_inc(sems[s], n)
                    c_own += own_inc
            return run
        blk.sync(runner("sync"))
        blk.tensor(runner("tensor"))
        blk.scalar(runner("scalar"))
        blk.vector(runner("vector"))
        blk.gpsimd(runner("gpsimd"))

    stack.close()
    return nc


_CACHE = {}


def kernel(**inputs):
    x = np.asarray(inputs["L0_attn_x"])
    hw = _host_weights(inputs)
    ht = _host_tables()
    shared = {}
    shared.update(hw)
    shared.update(ht)
    nc = build_program()
    in_maps = []
    for b in range(8):
        m = dict(shared)
        m["x"] = np.ascontiguousarray(x[b])
        in_maps.append(m)
    res = run_bass_kernel_spmd(nc, in_maps, core_ids=list(range(8)))
    out = np.stack([res.results[b]["out"] for b in range(8)], axis=0)
    return out


# version 13
# speedup vs baseline: 1.2305x; 1.0260x over previous
import math
import numpy as np
import ml_dtypes
from contextlib import ExitStack

import concourse.bass as bass
import concourse.mybir as mybir
from concourse.bass_utils import run_bass_kernel_spmd

F32 = mybir.dt.float32
BF16 = mybir.dt.bfloat16
I32 = mybir.dt.int32
AF = mybir.ActivationFunctionType
OP = mybir.AluOpType
AX = mybir.AxisListType
BF = ml_dtypes.bfloat16

S, D, NH, DN, DR, DV, C, I = 4096, 2048, 16, 128, 64, 128, 512, 8192
NT, TT = 8, 512
SCALE = 1.0 / math.sqrt(DN + DR)
LN_EPS = 1e-5
RMS_EPS = 1.1920928955078125e-07
MHAT_BIAS = math.log(SCALE) + 0.25 - 18.0

import os as _os
SUBJ = _os.environ.get("SUBJ", "")
ENGS = ("sync", "tensor", "scalar", "vector", "gpsimd")
OWNSEM = {"tensor": "pe", "scalar": "act", "vector": "dve", "gpsimd": "pool"}
JSEM = {"tensor": "jpe", "scalar": "jact", "vector": "jdve"}


def _host_tables():
    t = {}
    f = (10000.0 ** (-np.arange(0, DR, 2, dtype=np.float64) / DR))
    ang = np.arange(S, dtype=np.float64)[:, None] * f[None, :]
    cos, sin = np.cos(ang), np.sin(ang)
    cosq = np.empty((128, S), np.float32)
    sgnq = np.empty((128, S), np.float32)
    for r in range(128):
        j = r % 32
        cosq[r] = cos[:, j]
        sgnq[r] = (-sin[:, j]) if (r % 64) < 32 else sin[:, j]
    t["COSQ"] = cosq
    t["SGNQ"] = sgnq
    kt = np.empty((S, 64), np.float32)
    kt[:, 0:32] = cos
    kt[:, 32:64] = sin
    t["KTAB"] = kt
    t["IDT"] = np.eye(128, dtype=BF)
    t["TRI"] = np.triu(np.ones((128, 128), np.float32)).astype(BF)
    pswp = np.zeros((128, 128), np.float32)
    for r in range(128):
        pswp[r, r ^ 32] = 1.0
    t["PSWP"] = pswp.astype(BF)
    ehq = np.zeros((128, 24 * 16), np.float32)
    for oc in range(16):
        ehq[:, oc * 16 + oc] = 1.0
    for pc in range(8):
        oc = 16 + pc
        ehq[0:64, oc * 16 + 2 * pc] = 1.0
        ehq[64:128, oc * 16 + 2 * pc + 1] = 1.0
    t["EHQ"] = ehq.astype(BF)
    t["ONES64"] = np.ones((64, 1), np.float32).astype(BF)
    t["ONES1_16"] = np.ones((1, 16), np.float32).astype(BF)
    t["ONES1_128"] = np.ones((1, 128), np.float32).astype(BF)
    cst = np.zeros((128, 8), np.float32)
    cst[:, 0] = LN_EPS
    cst[:, 1] = -0.5
    cst[:, 2] = 1.0 / 512
    cst[:, 3] = RMS_EPS
    cst[:, 4] = 0.5
    cst[:, 5] = MHAT_BIAS
    cst[:, 6] = SCALE
    cst[:, 7] = -1.0
    t["CONSTS"] = cst
    return t


def _host_weights(inp):
    h = {}
    Wq = np.asarray(inp["L0_attn_Wq"])
    perm = []
    for hh in range(NH):
        perm += [hh * 192 + j for j in range(128)]
    for pc in range(8):
        for hh in (2 * pc, 2 * pc + 1):
            perm += [hh * 192 + 128 + 2 * j for j in range(32)]
            perm += [hh * 192 + 128 + 2 * j + 1 for j in range(32)]
    A = Wq[:, perm].reshape(16, 128, 24, 128)
    h["h_wq"] = np.ascontiguousarray(A.transpose(2, 1, 0, 3).reshape(24, 128, 2048))

    Wkva = np.asarray(inp["L0_attn_Wkv_a"])
    kperm = list(range(512)) + [512 + 2 * j for j in range(32)] + [512 + 2 * j + 1 for j in range(32)]
    B = Wkva[:, kperm].reshape(16, 128, 576)
    h["h_wkva"] = np.ascontiguousarray(B.transpose(1, 0, 2).reshape(128, 16 * 576))

    Wkvb = np.asarray(inp["L0_attn_Wkv_b"])
    kn_cols = [hh * 256 + j for hh in range(NH) for j in range(128)]
    v_cols = [hh * 256 + 128 + j for hh in range(NH) for j in range(128)]
    Bk = Wkvb[:, kn_cols].reshape(4, 128, 16, 128).transpose(2, 1, 0, 3)
    h["h_kbn"] = np.ascontiguousarray(Bk.reshape(16, 128, 512).reshape(4, 4, 128, 512)
                                      .transpose(0, 2, 1, 3).reshape(4, 128, 2048))
    h["h_kbv"] = np.ascontiguousarray(Wkvb[:, v_cols].reshape(4, 128, 2048))

    Wo = np.asarray(inp["L0_attn_Wo"])
    h["h_wo"] = np.ascontiguousarray(Wo.reshape(16, 128, 2048))

    gw = np.asarray(inp["L0_ffn_gate_w"])
    uw = np.asarray(inp["L0_ffn_up_w"])
    dw = np.asarray(inp["L0_ffn_down_w"])
    G = gw.reshape(64, 128, 16, 128)
    h["h_g"] = np.ascontiguousarray(
        G.transpose(0, 3, 2, 1).reshape(64, 128, 2048).view(np.uint8)[..., 0::4])
    U = uw.reshape(64, 128, 16, 128)
    h["h_u"] = np.ascontiguousarray(
        U.transpose(0, 3, 2, 1).reshape(64, 128, 2048).view(np.uint8)[..., 0::4])
    Dn = dw.reshape(2048, 64, 128).transpose(1, 2, 0)
    h["h_d"] = np.ascontiguousarray(np.ascontiguousarray(Dn).view(np.uint8)[..., 0::4])

    h["gsc"] = np.ascontiguousarray(np.asarray(inp["L0_ffn_gate_scale"]).reshape(64, 128).T)
    h["usc"] = np.ascontiguousarray(np.asarray(inp["L0_ffn_up_scale"]).reshape(64, 128).T)
    h["dscrep"] = np.ascontiguousarray(np.tile(np.asarray(inp["L0_ffn_down_scale"])[None, :], (128, 1)))
    h["wrep"] = np.ascontiguousarray(np.tile(np.asarray(inp["L0_attn_kv_norm_w"])[None, :], (128, 1)))
    return h


class Prog:
    def __init__(self, nc, stack):
        self.nc = nc
        self.q = {e: [] for e in ENGS}
        self.sems = {}
        self.cnt = {}
        self.hi = {e: {} for e in ENGS}
        self.pending = {}
        names = ["cv", "cvo", "pe", "act", "dve", "pool", "jpe", "jact", "jdve",
                 "tab0", "tab", "wkv", "krb", "krs"]
        for pfx in ("xb", "kt", "wq", "kbb", "vs", "kn", "vb",
                    "gb", "ub", "wd", "wo", "xr", "ks", "vt", "ot"):
            names += [pfx + "0", pfx + "1"]
        names += ["ot2", "ot3", "wd2"]
        for name in names:
            self.sems[name] = stack.enter_context(nc.semaphore(name))
            self.cnt[name] = 0

    def emit(self, eng, fn, waits=(), inc=None, inc_n=1, is_dma=False, fast=False):
        ws = []
        hw = self.hi[eng]
        for s, v in list(self.pending.pop(eng, [])) + list(waits):
            if v is None or v <= 0:
                continue
            if hw.get(s, -1) >= v:
                continue
            hw[s] = v
            ws.append((s, v))
        tok = None
        incs = []
        if inc is not None:
            self.cnt[inc] += inc_n
            tok = self.cnt[inc]
            incs.append((inc, inc_n))
        self.q[eng].append([fn, ws, incs, is_dma or fast])
        return tok

    def dma(self, eng, sem, out_ap, in_ap, waits=(), **kw):
        fn = (lambda e, o=out_ap, i=in_ap, kw=kw: e.dma_start(o, i, **kw))
        return self.emit(eng, fn, waits, inc=sem, inc_n=16, is_dma=True)

    def junction(self):
        for eng, js in JSEM.items():
            if self.q[eng]:
                self.emit(eng, lambda e: e.drain(), inc=js, is_dma=True)
        snap = [(s, self.cnt[s]) for s in ("jpe", "jact", "jdve")]
        for eng in ("sync", "tensor", "scalar", "vector"):
            self.pending[eng] = list(snap)


def build_program():
    nc = bass.Bass("TRN2", target_bir_lowering=False)
    stack = ExitStack()
    P = Prog(nc, stack)
    d = {}
    sb = {}

    def dram(name, shape, dt, kind="ExternalInput"):
        d[name] = nc.dram_tensor(name, list(shape), dt, kind=kind).ap()
        return d[name]

    def sbuf(name, shape, dt):
        sb[name] = stack.enter_context(nc.sbuf_tensor(name, list(shape), dt))
        return sb[name]

    dram("x", (S, D), F32)
    dram("h_wq", (24, 128, 2048), F32)
    dram("h_wkva", (128, 16 * 576), F32)
    dram("h_kbn", (4, 128, 2048), F32)
    dram("h_kbv", (4, 128, 2048), F32)
    dram("h_wo", (16, 128, 2048), F32)
    U8 = mybir.dt.uint8
    dram("h_g", (64, 128, 2048), U8)
    dram("h_u", (64, 128, 2048), U8)
    dram("h_d", (64, 128, 2048), U8)
    dram("gsc", (128, 64), F32)
    dram("usc", (128, 64), F32)
    dram("dscrep", (128, 2048), F32)
    dram("wrep", (128, 512), F32)
    dram("COSQ", (128, S), F32)
    dram("SGNQ", (128, S), F32)
    dram("KTAB", (S, 64), F32)
    dram("IDT", (128, 128), BF16)
    dram("TRI", (128, 128), BF16)
    dram("PSWP", (128, 128), BF16)
    dram("EHQ", (128, 384), BF16)
    dram("ONES64", (64, 1), BF16)
    dram("ONES1_16", (1, 16), BF16)
    dram("ONES1_128", (1, 128), BF16)
    dram("CONSTS", (128, 8), F32)
    dram("out", (S, D), F32, kind="ExternalOutput")
    dram("s_wq", (24, 128, 2048), BF16, kind="Internal")
    dram("s_wkva", (128, 16 * 576), BF16, kind="Internal")
    dram("s_kbn", (4, 128, 2048), BF16, kind="Internal")
    dram("s_kbv", (4, 128, 2048), BF16, kind="Internal")
    dram("s_wo", (2, 16, 128, 1024), BF16, kind="Internal")
    dram("s_g", (64, 128, 2048), BF16, kind="Internal")
    dram("s_u", (64, 128, 2048), BF16, kind="Internal")
    dram("s_wd", (2, 64, 128, 1024), BF16, kind="Internal")
    dram("s_kn", (16, 128, S), BF16, kind="Internal")
    dram("s_kr", (128, S), BF16, kind="Internal")
    dram("s_v", (16, 32, 128, 128), BF16, kind="Internal")

    sbuf("xnT", (128, 16 * 512), BF16)
    sbuf("OT", (128, 16 * 512), BF16)
    sbuf("cvb", (128, 8192), BF16)
    sbuf("dscb", (128, 2048), BF16)
    sbuf("IDTb", (128, 128), BF16)
    sbuf("TRIb", (128, 128), BF16)
    sbuf("PSWPb", (128, 128), BF16)
    sbuf("EHQb", (128, 384), BF16)
    sbuf("o64", (64, 1), BF16)
    sbuf("o116", (1, 16), BF16)
    sbuf("o1128", (1, 128), BF16)
    sbuf("gscb", (128, 64), F32)
    sbuf("uscb", (128, 64), F32)
    sbuf("wrepb", (128, 512), F32)
    sbuf("negm", (128, 16), F32)
    sbuf("krow", (1, 512), BF16)
    sbuf("st16", (16, 16), F32)
    sbuf("nmhb", (16, 1), BF16)
    sbuf("nmrow", (1, 16), BF16)
    sbuf("rl", (128, 8), F32)
    sbuf("cst", (128, 8), F32)
    sbuf("poolb", (128, 47104), BF16)
    sbuf("poolf", (128, 12304), F32)

    pb = [stack.enter_context(nc.psum_tensor(f"pb{i}", [128, 512], F32)) for i in range(6)]
    pbT_g = [nc.psum_tensor("pbT0_0", [128, 1024], BF16), nc.psum_tensor("pbT1_0", [128, 1024], BF16)]
    pbT = [g.__enter__() for g in pbT_g]

    def bview(off, n):
        return sb["poolb"][:, off:off + n]

    def fview(off, n):
        return sb["poolf"][:, off:off + n]

    QN_O, QR_O, CKT_O = 0, 8192, 12288
    WKVA_O, XNTOK_O, WQB_O, KBB_O = 14336, 23552, 25600, 29696
    SQST_O, KNST_O, KRSQ_O, KRST_O, VST_O, VSL_O = 30720, 31744, 32768, 33280, 33792, 35840
    KRB_O, KNB_O, VB_O, PT_O, OST_O = 14336, 18432, 26624, 34944, 36480
    HBUF_O, GB_O, UB_O, WDB_O, WOB_O, SGST_O, UST_O = 0, 32768, 36864, 40960, 44032, 46080, 46592
    XB_O, XSQ_O, STA_O = 0, 4096, 6144
    CKV_O, T512_O, QR32_O, TMA_O, TMB_O = 6160, 6736, 7248, 7760, 8272
    CSQ_O, SGQ_O, KT_O = 8784, 9296, 9808
    XR_O, OUTS_O = 6160, 8208

    E = P.emit
    DMA = P.dma
    sems = P.sems

    for nm, dst in (("IDT", sb["IDTb"]), ("TRI", sb["TRIb"]), ("PSWP", sb["PSWPb"]),
                    ("EHQ", sb["EHQb"]), ("ONES64", sb["o64"]), ("ONES1_16", sb["o116"]),
                    ("ONES1_128", sb["o1128"]), ("gsc", sb["gscb"]), ("usc", sb["uscb"]),
                    ("wrep", sb["wrepb"]), ("CONSTS", sb["cst"])):
        DMA("sync", "tab0", dst[:, :] if dst.shape[0] == 128 else dst[0:dst.shape[0], :],
            d[nm][0:dst.shape[0], :] if dst.shape[0] != 128 else d[nm][:, :])
    tok_tables = P.cnt["tab0"]
    E("vector", lambda e: e.memset(sb["st16"][0:16, 2:3], 0.0), inc="dve")
    tok_init_dve = P.cnt["dve"]

    cv_half = [sb["cvb"][:, i * 2048:(i + 1) * 2048] for i in range(4)]
    mark = {}

    DMA("gpsimd", "cv", sb["dscb"][:, :], d["dscrep"][:, :])

    units = []
    for k in range(5):
        n = 2048 if k < 4 else 1024
        units.append(("f32", d["h_wkva"][:, k * 2048:k * 2048 + n],
                      [(d["s_wkva"][:, k * 2048:k * 2048 + n], n)], n, "wkva"))
    for oc in range(24):
        units.append(("f32", d["h_wq"][oc], [(d["s_wq"][oc], 2048)], 2048, "wq"))
    for g in range(4):
        units.append(("f32", d["h_kbn"][g], [(d["s_kbn"][g], 2048)], 2048, "kb"))
    for g in range(4):
        units.append(("f32", d["h_kbv"][g], [(d["s_kbv"][g], 2048)], 2048, "kb"))
    for ic in range(64):
        units.append(("int", d["h_g"][ic], [(d["s_g"][ic], 2048)], 2048, ("g", ic)))
        units.append(("int", d["h_u"][ic], [(d["s_u"][ic], 2048)], 2048, ("u", ic)))
    for ic in range(64):
        units.append(("wd", d["h_d"][ic],
                      [(d["s_wd"][0, ic], 1024), (d["s_wd"][1, ic], 1024)], 2048, ("wd", ic)))
    for hh in range(16):
        units.append(("f32", d["h_wo"][hh],
                      [(d["s_wo"][0, hh], 1024), (d["s_wo"][1, hh], 1024)], 2048, ("wo", hh)))

    NU = len(units)
    half_free = {0: None, 1: None, 2: None, 3: None}
    cv_tok = {}
    LOOKAHEAD = 2

    def hw(h):
        f = half_free[h]
        return [f] if f else []

    def conv_prep(i):
        kind, srcap, dsts, n, mk = units[i]
        h = i % 4
        if kind == "f32":
            cv_tok[i] = DMA("gpsimd", "cv", cv_half[h][:, 0:n], srcap, waits=hw(h))
        else:
            tm = E("gpsimd", lambda e, a=cv_half[h][:, 0:n]: e.memset(a, -128.0),
                   waits=hw(h), inc="pool")
            cv_tok[i] = DMA("gpsimd", "cv", cv_half[h][:, 0:n], srcap,
                            waits=[("pool", tm)], accum_op=OP.add)

    def conv_store(i):
        kind, srcap, dsts, n, mk = units[i]
        h = i % 4
        srcbuf = cv_half[h]
        w = [("cv", cv_tok[i])]
        if kind == "wd":
            tt = E("gpsimd", lambda e, o=cv_half[h][:, 0:n], a=cv_half[h][:, 0:n]:
                   e.tensor_mul(o, a, sb["dscb"][:, 0:n]),
                   waits=w, inc="pool")
            w = [("pool", tt)]
        tok = None
        off = 0
        for ds, wd_ in dsts:
            tok = DMA("gpsimd", "cvo", ds, srcbuf[:, off:off + wd_], waits=w)
            off += wd_
        half_free[h] = ("cvo", tok)
        mark[mk] = tok

    for i in range(min(LOOKAHEAD, NU)):
        conv_prep(i)
    for i in range(NU):
        if i + LOOKAHEAD < NU:
            conv_prep(i + LOOKAHEAD)
        conv_store(i)

    cvo_final = P.cnt["cvo"]
    for k in list(mark):
        mark[k] = min(mark[k] + 96, cvo_final)

    state = {"pbT": pbT, "pbT_g": pbT_g}

    def mm(out_ap, lhsT, rhs, start, stop, waits=(), inc="pe"):
        return E("tensor",
                 lambda e, o=out_ap, l=lhsT, r=rhs, s1=start, s2=stop: e.matmul(o, l, r, start=s1, stop=s2),
                 [("tab0", tok_tables)] + list(waits), inc=inc)

    def transpose(out_ap, in_ap, waits=()):
        return E("tensor",
                 lambda e, o=out_ap, i=in_ap: e.transpose(o, i, sb["IDTb"][:, :]),
                 [("tab0", tok_tables)] + list(waits), inc="pe")

    def nop(eng, waits=()):
        return E(eng, lambda e: e.nop(cycle_cnt=384), waits, is_dma=True)

    def act(fn, waits=(), fast=False):
        return E("scalar", fn, [("tab0", tok_tables)] + list(waits), inc="act", fast=fast)

    def dve(fn, waits=(), fast=False):
        return E("vector", fn, [("tab0", tok_tables)] + list(waits), inc="dve", fast=fast)

    xnT = [sb["xnT"][:, c * 512:(c + 1) * 512] for c in range(16)]
    OTv = [sb["OT"][:, hh * 512:(hh + 1) * 512] for hh in range(16)]
    stats = sb["st16"]

    for qt in range(NT):
        T0 = qt * TT
        nkc = 4 * (qt + 1)
        kend = T0 + TT

        ocs = state.get(("ot_prev",), (0, 0, 0, 0))
        DMA("sync", "tab", fview(CSQ_O, 512), d["COSQ"][:, T0:kend],
            waits=[("dve", state.get(("evac_prev",), 0))] +
                  [(f"ot{j}", ocs[j]) for j in range(4)])
        tok_sgnq = tok_cosq = DMA("sync", "tab", fview(SGQ_O, 512), d["SGNQ"][:, T0:kend])
        tok_wkva = DMA("sync", "wkv", bview(WKVA_O, 9216), d["s_wkva"][:, :],
                       waits=[("cvo", mark["wkva"])])

        tok_xnT = {}
        tok_ckvnT = {}
        tok_x, tok_kt = [], []
        last_kr_evac = None
        for j in range(4):
            pre = state.pop(("tok_x_pre", qt, j), None)
            tok_x.append(pre if pre is not None else
                         DMA("sync", f"xb{j % 2}", fview(XB_O + (j % 2) * 2048, 2048),
                             d["x"][T0 + j * 128:T0 + (j + 1) * 128, :],
                             waits=[("dve", state.get(("xbfree", j % 2), 0))]))
            tok_kt.append(DMA("sync", f"kt{j % 2}", fview(KT_O + (j % 2) * 64, 64),
                              d["KTAB"][T0 + j * 128:T0 + (j + 1) * 128, :],
                              waits=[("dve", state.get(("ktab_free", j % 2), 0))]))
            xb = fview(XB_O + (j % 2) * 2048, 2048)
            xsq = fview(XSQ_O, 2048)
            st = fview(STA_O, 16)
            act(lambda e, o=xsq, i=xb: e.activation(o, i, AF.Copy, accum_out=st[:, 0:1]),
                waits=[(f"xb{j % 2}", tok_x[j])])
            tA = act(lambda e, o=xsq, i=xb: e.activation(o, i, AF.Square, accum_out=st[:, 1:2]))
            dve(lambda e, o=st[:, 2:3], i=st[:, 0:1]: e.tensor_scalar_mul(o, i, 1.0 / 2048),
                waits=[("act", tA)])
            dve(lambda e, o=st[:, 3:4], i=st[:, 1:2]: e.tensor_scalar_mul(o, i, 1.0 / 2048))
            dve(lambda e, o=st[:, 4:5], a=st[:, 2:3]: e.tensor_mul(o, a, a))
            t3 = dve(lambda e, o=st[:, 5:6], a=st[:, 3:4], b=st[:, 4:5]: e.tensor_sub(o, a, b))
            nop("scalar", waits=[("dve", t3)])
            act(lambda e, o=st[:, 6:7], i=st[:, 5:6]: e.activation(o, i, AF.Ln, bias=sb["cst"][:, 0:1]))
            tR = act(lambda e, o=st[:, 7:8], i=st[:, 6:7]: e.activation(o, i, AF.Exp, scale=sb["cst"][:, 1:2]))
            tN = dve(lambda e, o=st[:, 8:9], i=st[:, 2:3]: e.tensor_scalar_mul(o, i, -1.0))
            xntok = bview(XNTOK_O, 2048)
            nop("vector", waits=[("act", tR)])
            tXN = dve(lambda e, o=xntok, i=xb, s1=st[:, 8:9], s2=st[:, 7:8]:
                      e.tensor_scalar(o, i, s1, s2, OP.add, OP.mult),
                      waits=[("pe", state.get(("xntok_free",), 0)),
                             ("pe", state.get(("tokR_prev",), 0))])
            state[("xbfree", j % 2)] = tXN
            for c in range(16):
                tp = transpose(state["pbT"][c % 2][:, 0:128], xntok[:, c * 128:(c + 1) * 128],
                               waits=[("dve", tXN if c == 0 else 0),
                                      ("act", 0 if c == 0 else state.get(("pbT_free", c % 2), 0))])
                if c == 15:
                    state[("xntok_free",)] = tp
                te = act(lambda e, o=xnT[c][:, j * 128:(j + 1) * 128], i=state["pbT"][c % 2][:, 0:128]:
                         e.copy(o, i), waits=[("pe", tp)], fast=True)
                state[("pbT_free", c % 2)] = te
                tok_xnT[(j, c)] = te
            w = [("wkv", tok_wkva), ("act", tok_xnT[(j, 15)]),
                 ("dve", state.get(("ckv_free", qt, j - 1), 0))]
            for dc in range(16):
                wv = w if dc == 0 else ()
                mm(pb[2][:, :], xnT[dc][:, j * 128:(j + 1) * 128],
                   bview(WKVA_O + dc * 576, 512), dc == 0, dc == 15, waits=wv, inc=None)
            for dc in range(16):
                tkva = mm(pb[3][:, 0:64], xnT[dc][:, j * 128:(j + 1) * 128],
                          bview(WKVA_O + dc * 576 + 512, 64), dc == 0, dc == 15,
                          inc="pe" if dc == 15 else None)
            ckv = fview(CKV_O, 576)
            ocs = state.get(("ot_prev",), (0, 0, 0, 0))
            tc1 = dve(lambda e, o=ckv[:, 0:512], i=pb[2][:, :]: e.tensor_copy(o, i),
                      waits=[("pe", tkva)] + [(f"ot{j2}", ocs[j2]) for j2 in range(4)])
            tc2 = dve(lambda e, o=ckv[:, 512:576], i=pb[3][:, 0:64]: e.tensor_copy(o, i))
            state[("ckv_free", qt, j)] = tc2
            kt = fview(KT_O + (j % 2) * 64, 64)
            tma, tmb = fview(TMA_O, 512), fview(TMB_O, 512)
            krt = bview(KRST_O, 512)
            dve(lambda e, o=tma[:, 0:32], a=ckv[:, 512:544], b=kt[:, 0:32]: e.tensor_mul(o, a, b),
                waits=[(f"kt{j % 2}", tok_kt[j])])
            dve(lambda e, o=tmb[:, 0:32], a=ckv[:, 544:576], b=kt[:, 32:64]: e.tensor_mul(o, a, b))
            dve(lambda e, o=krt[:, 0:32], a=tma[:, 0:32], b=tmb[:, 0:32]: e.tensor_sub(o, a, b),
                waits=[("pe", state.get(("krt_free",), 0))])
            dve(lambda e, o=tma[:, 0:32], a=ckv[:, 544:576], b=kt[:, 0:32]: e.tensor_mul(o, a, b))
            tktf = dve(lambda e, o=tmb[:, 0:32], a=ckv[:, 512:544], b=kt[:, 32:64]: e.tensor_mul(o, a, b))
            state[("ktab_free", j % 2)] = tktf
            tkr = dve(lambda e, o=krt[:, 32:64], a=tma[:, 0:32], b=tmb[:, 0:32]: e.tensor_add(o, a, b))
            t512 = fview(T512_O, 512)
            tss = act(lambda e, o=t512, i=ckv[:, 0:512]:
                      e.activation(o, i, AF.Square, accum_out=st[:, 9:10]),
                      waits=[("dve", tc2)])
            act(lambda e, o=st[:, 10:11], i=st[:, 9:10]:
                e.activation(o, i, AF.Ln, scale=sb["cst"][:, 2:3], bias=sb["cst"][:, 3:4]),
                waits=[("act", tss)])
            tr2 = act(lambda e, o=st[:, 11:12], i=st[:, 10:11]:
                      e.activation(o, i, AF.Exp, scale=sb["cst"][:, 1:2]))
            ckvnb = bview(KNST_O, 512)
            ckvnb = bview(SQST_O, 512)
            nop("vector", waits=[("act", tr2)])
            tck = dve(lambda e, o=ckvnb, a=ckv[:, 0:512], s=st[:, 11:12]:
                      e.scalar_tensor_tensor(o, a, s, sb["wrepb"][:, :], OP.mult, OP.mult),
                      waits=[("pe", state.get(("ckvnb_free",), 0))])
            for cc in range(4):
                tp = transpose(state["pbT"][cc % 2][:, 0:128], ckvnb[:, cc * 128:(cc + 1) * 128],
                               waits=[("dve", tck), ("act", state.get(("pbT_free", cc % 2), 0))])
                te = act(lambda e, o=bview(CKT_O + cc * 512, 512)[:, j * 128:(j + 1) * 128],
                         i=state["pbT"][cc % 2][:, 0:128]: e.copy(o, i), waits=[("pe", tp)], fast=True)
                state[("pbT_free", cc % 2)] = te
                tok_ckvnT[(j, cc)] = te
            state[("ckvnb_free",)] = tp
            nop("tensor", waits=[("dve", tkr)])
            tp = transpose(state["pbT"][0][0:64, 0:128], krt[:, 0:64],
                           waits=[("act", state.get(("pbT_free", 0), 0))])
            state[("krt_free",)] = tp
            te = act(lambda e, o=bview(KRSQ_O, 512)[0:64, j * 128:(j + 1) * 128],
                     i=state["pbT"][0][0:64, 0:128]: e.copy(o, i), waits=[("pe", tp)], fast=True)
            state[("pbT_free", 0)] = te
            last_kr_evac = te

        if "a" in SUBJ:
            P.junction()
        krstage = bview(KRSQ_O, 512)
        DMA("scalar", "krs", d["s_kr"][0:64, T0:kend], krstage[0:64, :],
            waits=[("act", last_kr_evac)])
        tok_kr_st = DMA("scalar", "krs", d["s_kr"][64:128, T0:kend], krstage[0:64, :])

        sqst = [bview(KRST_O, 512), bview(VST_O, 512)]
        qn = [bview(QN_O + oc * 512, 512) for oc in range(16)]
        qr = [bview(QR_O + pc * 512, 512) for pc in range(8)]
        tok_sq = {}
        tok_normq = {}
        tok_wq = {}
        for oc in range(24):
            tok_wq[oc] = DMA("sync", f"wq{oc % 2}", bview(WQB_O + (oc % 2) * 2048, 2048), d["s_wq"][oc],
                             waits=[("cvo", mark["wq"]),
                                    ("pe", state.get(("wq_free", qt, oc - 2), 0))])
            bank = pb[oc % 2]
            w = [(f"wq{oc % 2}", tok_wq[oc]), ("act", state.get(("qevac", oc - 2), 0)),
                 ("dve", state.get(("qrope_done", oc - 2), 0))]
            for dc in range(16):
                tq = mm(bank[:, :], bview(WQB_O + (oc % 2) * 2048, 2048)[:, dc * 128:(dc + 1) * 128],
                        xnT[dc], dc == 0, dc == 15, waits=(w if dc == 0 else ()),
                        inc="pe" if dc == 15 else None)
            state[("wq_free", qt, oc)] = tq
            tsq = act(lambda e, o=sqst[oc % 2], i=bank[:, :]: e.activation(o, i, AF.Square),
                      waits=[("pe", tq), ("pe", tok_normq.get(oc - 2, 0))])
            tok_sq[oc] = tsq
            if oc < 16:
                tev = act(lambda e, o=qn[oc], i=bank[:, :]: e.copy(o, i), fast=True)
            else:
                pc = oc - 16
                qr32 = fview(QR32_O, 512)
                act(lambda e, o=qr32, i=bank[:, :]: e.copy(o, i))
                tev = act(lambda e, o=qr[pc], i=bank[:, :]: e.copy(o, i), fast=True)
                nop("tensor", waits=[("act", tev)])
                tpm = mm(pb[5][:, :], sb["PSWPb"][:, :], qr[pc], True, True,
                         waits=[("dve", state.get(("pb5_free",), 0))])
                tma, tmb = fview(TMA_O, 512), fview(TMB_O, 512)
                t1 = dve(lambda e, o=tma, a=pb[5][:, :], b=fview(SGQ_O, 512): e.tensor_mul(o, a, b),
                         waits=[("pe", tpm), ("tab", tok_sgnq)])
                state[("pb5_free",)] = t1
                dve(lambda e, o=tmb, a=qr32, b=fview(CSQ_O, 512): e.tensor_mul(o, a, b),
                    waits=[("tab", tok_cosq)])
                trr = dve(lambda e, o=qr[pc], a=tma, b=tmb: e.tensor_add(o, a, b))
                state[("qrope_done", oc)] = trr
            state[("qevac", oc)] = tev
            tok_normq[oc] = mm(pb[4][0:16, :], sb["EHQb"][:, oc * 16:(oc + 1) * 16],
                               sqst[oc % 2], oc == 0, oc == 23, waits=[("act", tsq)])
        if "a" in SUBJ:
            P.junction()
        tmq = dve(lambda e, o=stats[0:16, 0:1], i=pb[4][0:16, :]: e.reduce_max(o, i, AX.X),
                  waits=[("pe", tok_normq[23])])

        tok_kn_st = None
        tok_normk = {}
        tok_kb = {}
        for oc in range(16):
            g, r = oc // 4, oc % 4
            tok_kb[oc] = DMA("sync", f"kbb{oc % 2}", bview(KBB_O + (oc % 2) * 512, 512),
                             d["s_kbn"][g][:, r * 512:(r + 1) * 512],
                             waits=[("cvo", mark["kb"]),
                                    ("pe", state.get(("kb_free", qt, oc - 2), 0))])
            if oc == 0:
                nop("tensor", waits=[("act", tok_ckvnT[(3, 3)])])
            bank = pb[oc % 2]
            w = [(f"kbb{oc % 2}", tok_kb[oc]), ("act", state.get(("knevac", oc - 2), 0)),
                 ("act", tok_ckvnT[(3, 3)])]
            for cc in range(4):
                tk = mm(bank[:, :], bview(KBB_O + (oc % 2) * 512, 512)[:, cc * 128:(cc + 1) * 128],
                        bview(CKT_O + cc * 512, 512), cc == 0, cc == 3, waits=(w if cc == 0 else ()),
                        inc="pe" if cc == 3 else None)
            state[("kb_free", qt, oc)] = tk
            tsq = act(lambda e, o=sqst[oc % 2], i=bank[:, :]: e.activation(o, i, AF.Square),
                      waits=[("pe", tk), ("pe", tok_normk.get(oc - 2, 0)),
                             ("dve", tmq if oc == 0 else 0)])
            tev = act(lambda e, o=bview(KNST_O + (oc % 2) * 512, 512), i=bank[:, :]: e.copy(o, i), fast=True,
                      waits=[(f"ks{oc % 2}", state.get(("knst_st", oc - 2), 0))])
            state[("knevac", oc)] = tev
            tok_kn_st = DMA("scalar", f"ks{oc % 2}", d["s_kn"][oc][:, T0:kend],
                            bview(KNST_O + (oc % 2) * 512, 512))
            state[("knst_st", oc)] = tok_kn_st
            tok_normk[oc] = mm(pb[4][0:16, :], sb["EHQb"][:, oc * 16:(oc + 1) * 16],
                               sqst[oc % 2], oc == 0, False, waits=[("act", tsq)])
        tkq = act(lambda e, o=bview(KRST_O, 512)[0:64, :], i=krstage[0:64, :]:
                  e.activation(o, i, AF.Square), waits=[("pe", tok_normk[15])])
        tks = mm(pb[5][0:1, :], sb["o64"][0:64, 0:1], bview(KRST_O, 512)[0:64, :], True, True,
                 waits=[("act", tkq), ("dve", state.get(("pb5_free",), 0))])
        tkr2 = act(lambda e, o=sb["krow"][0:1, :], i=pb[5][0:1, :]: e.copy(o, i), waits=[("pe", tks)])
        tbc = mm(pb[4][0:16, :], sb["o116"][0:1, 0:16], sb["krow"][0:1, :], False, True,
                 waits=[("act", tkr2)])
        t1 = dve(lambda e, o=stats[0:16, 1:2], i=pb[4][0:16, :]: e.reduce_max(o, i, AX.X),
                 waits=[("pe", tbc)])
        tmk = dve(lambda e, o=stats[0:16, 2:3], a=stats[0:16, 2:3], b=stats[0:16, 1:2]:
                  e.tensor_max(o, a, b))
        nop("scalar", waits=[("dve", tmk)])
        act(lambda e, o=stats[0:16, 3:4], i=stats[0:16, 0:1]: e.activation(o, i, AF.Ln))
        tl2 = act(lambda e, o=stats[0:16, 4:5], i=stats[0:16, 2:3]: e.activation(o, i, AF.Ln))
        nop("vector", waits=[("act", tl2)])
        tls = dve(lambda e, o=stats[0:16, 5:6], a=stats[0:16, 3:4], b=stats[0:16, 4:5]:
                  e.tensor_add(o, a, b))
        nop("scalar", waits=[("dve", tls)])
        tmh = act(lambda e, o=stats[0:16, 6:7], i=stats[0:16, 5:6]:
                  e.activation(o, i, AF.Exp, scale=sb["cst"][0:16, 4:5], bias=sb["cst"][0:16, 5:6]))
        tnm = dve(lambda e, o=stats[0:16, 7:8], i=stats[0:16, 6:7]: e.tensor_scalar_mul(o, i, -1.0),
                  waits=[("act", tmh)])
        nop("scalar", waits=[("dve", tnm)])
        tnb = act(lambda e, o=sb["nmhb"][0:16, 0:1], i=stats[0:16, 7:8]: e.copy(o, i))
        nop("tensor", waits=[("act", tnb)])
        ttr = E("tensor", lambda e: e.transpose(state["pbT"][1][0:1, 0:16], sb["nmhb"][0:16, 0:1],
                                                sb["IDTb"][0:16, 0:16]),
                waits=[("act", state.get(("pbT_free", 1), 0))], inc="pe")
        trw = act(lambda e, o=sb["nmrow"][0:1, 0:16], i=state["pbT"][1][0:1, 0:16]: e.copy(o, i),
                  waits=[("pe", ttr)])
        state[("pbT_free", 1)] = trw
        nop("tensor", waits=[("act", trw)])
        tbm = mm(pb[5][:, 0:16], sb["o1128"][0:1, :], sb["nmrow"][0:1, 0:16], True, True)
        tok_negm = act(lambda e, o=sb["negm"][:, :], i=pb[5][:, 0:16]: e.copy(o, i),
                       waits=[("pe", tbm)])

        if "a" in SUBJ:
            P.junction()
        tok_v_st = None
        tok_vsl = {}
        for vg in range(4):
            tok_vsl[vg] = DMA("sync", f"vs{vg % 2}",
                              bview(VSL_O + (vg % 2) * 2048, 2048).rearrange("p (a b) -> p a b", b=512),
                              d["s_kbv"].transpose([1, 0, 2])[:, :, vg * 512:(vg + 1) * 512],
                              waits=[("cvo", mark["kb"]),
                                     ("pe", state.get(("vsl_free", qt, vg - 2), 0))])
            tv = None
            for j in range(4):
                bank = pb[2 + (j % 2)]
                w = [(f"vs{vg % 2}", tok_vsl[vg]), ("act", state.get(("vbank_free", j % 2), 0)),
                     ("dve", state[("ckv_free", qt, 3)]),
                     ("act", tok_ckvnT[(3, 3)])]
                for cc in range(4):
                    tv = mm(bank[:, :], bview(CKT_O + cc * 512, 512)[:, j * 128:(j + 1) * 128],
                            bview(VSL_O + (vg % 2) * 2048, 2048)[:, cc * 512:(cc + 1) * 512],
                            cc == 0, cc == 3, waits=(w if cc == 0 else ()),
                            inc="pe" if cc == 3 else None)
                vst = bview(VST_O + (j % 2) * 512, 512)
                tve = act(lambda e, o=vst, i=bank[:, :]: e.copy(o, i), fast=True,
                          waits=[("pe", tv), (f"vt{j % 2}", state.get(("vst_st", j % 2), 0))])
                state[("vbank_free", j % 2)] = tve
                for hx in range(4):
                    hh = vg * 4 + hx
                    tok_v_st = DMA("scalar", f"vt{j % 2}", d["s_v"][hh, 4 * qt + j],
                                   vst[:, hx * 128:(hx + 1) * 128], waits=[("act", tve)])
                state[("vst_st", j % 2)] = tok_v_st
            state[("vsl_free", qt, vg)] = tv

        P.junction()

        krb = bview(KRB_O, 4096)
        tok_krb = DMA("sync", "krb", krb[:, 0:kend], d["s_kr"][:, 0:kend],
                      waits=[("ks0", P.cnt["ks0"]), ("ks1", P.cnt["ks1"]),
                             ("vt0", P.cnt["vt0"]), ("vt1", P.cnt["vt1"]),
                             ("krs", P.cnt["krs"])])
        tok_vones = None
        for p_ in range(2):
            vb_ones = bview(VB_O + p_ * 4160, 4160).rearrange("p (k c) -> p k c", c=130)[:, :, 128:130]
            tok_vones = dve(lambda e, a=vb_ones: e.memset(a, 1.0))
        pvdone = {}
        scdone = {}
        sb_rot = [0, 1, 4]
        sbank_free = {0: 0, 1: 0, 2: 0}
        pt_free = {0: 0, 1: 0, 2: 0}
        for hh in range(16):
            p = hh % 2
            tok_knb = DMA("sync", f"kn{p}", bview(KNB_O + p * 4096, 4096)[:, 0:kend],
                          d["s_kn"][hh][:, 0:kend],
                          waits=[("pe", scdone.get(hh - 2, 0))])
            tok_vb = DMA("sync", f"vb{p}",
                         bview(VB_O + p * 4160, 4160).rearrange("p (k c) -> p k c", c=130)[:, 0:nkc, 0:128],
                         d["s_v"][hh].transpose([1, 0, 2])[:, 0:nkc, :],
                         waits=[("pe", pvdone.get(hh - 2, 0))])
            knb = bview(KNB_O + p * 4096, 4096)
            vbv = bview(VB_O + p * 4160, 4160)
            qrv = bview(QR_O + (hh // 2) * 512, 512)
            r0 = (hh % 2) * 64
            tok_exp, tok_tri = {}, {}

            def scores(kc):
                qoff = max(0, (kc - 4 * qt) * 128)
                bank = pb[sb_rot[kc % 3]]
                w = [(f"kn{p}", tok_knb), ("krb", tok_krb),
                     ("act", sbank_free[kc % 3])]
                mm(bank[:, 0:TT - qoff], knb[:, kc * 128:(kc + 1) * 128],
                   qn[hh][:, qoff:TT], True, False, waits=w, inc=None)
                return mm(bank[:, 0:TT - qoff], krb[r0:r0 + 64, kc * 128:(kc + 1) * 128],
                          qrv[r0:r0 + 64, qoff:TT], False, True)

            def expmask(kc, tsc):
                qoff = max(0, (kc - 4 * qt) * 128)
                bank = pb[sb_rot[kc % 3]]
                pt = bview(PT_O + (kc % 3) * 512, 512)
                te = act(lambda e, o=pt[:, 0:TT - qoff], i=bank[:, 0:TT - qoff]:
                         e.activation(o, i, AF.Exp, scale=sb["cst"][:, 6:7], bias=sb["negm"][:, hh:hh + 1]),
                         waits=[("pe", tsc), ("pe", pt_free[kc % 3])], fast=True)
                tok_exp[kc] = te
                sbank_free[kc % 3] = te
                if kc >= 4 * qt:
                    tok_tri[kc] = dve(lambda e, o=pt[:, 0:128], a=pt[:, 0:128]:
                                      e.tensor_mul(o, a, sb["TRIb"][:, :]), waits=[("act", te)], fast=True)

            def pv_one(kc, pq):
                qoff = max(0, (kc - 4 * qt) * 128)
                p0 = max(0, kc - 4 * qt)
                pt = bview(PT_O + (kc % 3) * 512, 512)
                col = pq * 128 - qoff
                bank = pb[2 + pq // 2]
                diag = (pq == p0 and kc >= 4 * qt)
                if diag and kc == 0:
                    nop("tensor", waits=[("dve", tok_tri[kc])])
                w = [(f"vb{p}", tok_vb), ("dve", tok_vones),
                     ("dve", tok_tri.get(kc, 0) if diag else 0),
                     ("act", tok_exp[kc]),
                     ("act", state.get(("pO_free", pq // 2), 0) if kc == 0 else 0)]
                start = (kc == 0 and pq % 2 == 0)
                stop = (pq % 2 == 1 and kc == 4 * qt + pq)
                t = mm(bank[:, (pq % 2) * 129:(pq % 2) * 129 + 129],
                       pt[:, col:col + 128], vbv[:, kc * 130:kc * 130 + 129],
                       start, stop, waits=w)
                pt_free[kc % 3] = t
                return t

            def pv(kc, deferred):
                p0 = max(0, kc - 4 * qt)
                tlast = None
                for dq in deferred:
                    tlast = pv_one(*dq)
                deferred.clear()
                if kc >= 4 * qt and kc > 0:
                    deferred.append((kc, p0))
                    lo = p0 + 1
                else:
                    lo = p0
                for pq in range(lo, 4):
                    tlast = pv_one(kc, pq)
                return tlast

            if "b" in SUBJ:
                P.junction()
            nk = nkc
            deferred = []
            tpv_last = None
            ts_prev = scores(0)
            for kc in range(1, nk):
                ts = scores(kc)
                expmask(kc - 1, ts_prev)
                t = pv(kc - 1, deferred)
                tpv_last = t if t is not None else tpv_last
                ts_prev = ts
            expmask(nk - 1, ts_prev)
            t = pv(nk - 1, deferred)
            tpv_last = t if t is not None else tpv_last
            for dq in deferred:
                tpv_last = pv_one(*dq)
            deferred.clear()
            scdone[hh] = ts_prev
            pvdone[hh] = tpv_last

            toknorm = {}
            import concourse.bass as _bass
            for bk in range(2):
                bank = pb[2 + bk]
                ells = _bass.AP(bank.tensor, 128, [[bank.tensor.shape[1], 128], [129, 2]]) \
                    if False else bank[:, 128:258:129]
                act(lambda e, o=sb["rl"][:, bk * 4:bk * 4 + 2], i=ells:
                    e.activation(o, i, AF.Ln), waits=[("pe", tpv_last)])
                act(lambda e, o=sb["rl"][:, bk * 4 + 2:bk * 4 + 4],
                    i=sb["rl"][:, bk * 4:bk * 4 + 2]:
                    e.activation(o, i, AF.Exp, scale=sb["cst"][:, 7:8]))
            for pq in range(4):
                bank = pb[2 + pq // 2]
                ov = bank[:, (pq % 2) * 129:(pq % 2) * 129 + 128]
                tn = act(lambda e, o=bview(OST_O + pq * 128, 128), i=ov,
                         s=sb["rl"][:, (pq // 2) * 4 + 2 + (pq % 2):(pq // 2) * 4 + 3 + (pq % 2)]:
                         e.activation(o, i, AF.Copy, scale=s),
                         waits=[("pe", state.get(("ost_free", pq), 0))], fast=True)
                toknorm[pq] = tn
                state[("pO_free", pq // 2)] = tn
            for pq in range(4):
                tp = transpose(state["pbT"][pq % 2][:, 0:128], bview(OST_O + pq * 128, 128),
                               waits=[("act", toknorm[pq]), ("act", state.get(("pbT_free", pq % 2), 0))])
                te = act(lambda e, o=OTv[hh][:, pq * 128:(pq + 1) * 128],
                         i=state["pbT"][pq % 2][:, 0:128]: e.copy(o, i), waits=[("pe", tp)], fast=True)
                state[("pbT_free", pq % 2)] = te
                state[("ost_free", pq)] = tp

        P.junction()

        for g in reversed(state["pbT_g"]):
            g.__exit__(None, None, None)
        pb67_g = [nc.psum_tensor(f"pb6_{qt}", [128, 512], F32),
                  nc.psum_tensor(f"pb7_{qt}", [128, 512], F32)]
        pb67 = [g.__enter__() for g in pb67_g]
        pbR = pb + pb67

        if qt + 1 < NT:
            for j in range(2):
                state[("tok_x_pre", qt + 1, j)] = DMA(
                    "sync", f"xb{j}", fview(XB_O + j * 2048, 2048),
                    d["x"][(qt + 1) * TT + j * 128:(qt + 1) * TT + (j + 1) * 128, :],
                    waits=[("dve", state.get(("xbfree", j), 0))])

        hb = [bview(HBUF_O + ic * 512, 512) for ic in range(64)]
        tok_h = {}
        for ic in range(64):
            p = ic % 2
            tg = DMA("sync", f"gb{p}", bview(GB_O + p * 2048, 2048), d["s_g"][ic],
                     waits=[("cvo", mark[("g", ic)]), ("pe", state.get(("gb_free", qt, ic - 2), 0))])
            tu = DMA("sync", f"ub{p}", bview(UB_O + p * 2048, 2048), d["s_u"][ic],
                     waits=[("cvo", mark[("u", ic)]), ("pe", state.get(("ub_free", qt, ic - 2), 0))])
            gbank, ubank = pb[p], pb[2 + p]
            w = [(f"gb{p}", tg), ("act", state.get(("sg_done", ic - 2), 0))]
            for dc in range(16):
                tgm = mm(gbank[:, :], bview(GB_O + p * 2048, 2048)[:, dc * 128:(dc + 1) * 128],
                         xnT[dc], dc == 0, dc == 15, waits=(w if dc == 0 else ()),
                         inc="pe" if dc == 15 else None)
            state[("gb_free", qt, ic)] = tgm
            w = [(f"ub{p}", tu), ("act", state.get(("us_done", ic - 2), 0))]
            for dc in range(16):
                tum = mm(ubank[:, :], bview(UB_O + p * 2048, 2048)[:, dc * 128:(dc + 1) * 128],
                         xnT[dc], dc == 0, dc == 15, waits=(w if dc == 0 else ()),
                         inc="pe" if dc == 15 else None)
            state[("ub_free", qt, ic)] = tum
            sg = bview(SGST_O, 512)
            us = bview(UST_O, 512)
            tsg = act(lambda e, o=sg, i=gbank[:, :], s=sb["gscb"][:, ic:ic + 1]:
                      e.activation(o, i, AF.Silu, scale=s),
                      waits=[("pe", tgm), ("dve", tok_h.get(ic - 1, 0))], fast=True)
            state[("sg_done", ic)] = tsg
            tus = act(lambda e, o=us, i=ubank[:, :], s=sb["uscb"][:, ic:ic + 1]:
                      e.activation(o, i, AF.Copy, scale=s), waits=[("pe", tum)], fast=True)
            state[("us_done", ic)] = tus
            tok_h[ic] = dve(lambda e, o=hb[ic], a=sg, b=us: e.tensor_mul(o, a, b),
                            waits=[("act", tus)], fast=True)

        if "c" in SUBJ:
            P.junction()
        for egh in range(2):
            tok_R = None
            for ic in range(64):
                p = ic % 3
                wprev = (state.get(("wdb_free", qt, egh, ic - 3), 0) if ic >= 3
                         else state.get(("wdb_free", qt, egh - 1, 61 + ic), 0))
                tw = DMA("sync", f"wd{p}", bview(WDB_O + p * 1024, 1024), d["s_wd"][egh, ic],
                         waits=[("cvo", mark[("wd", ic)]), ("pe", wprev),
                                ("dve", state.get(("evac_done", qt, egh - 1), 0) if ic == 0 else 0)])
                w0 = [(f"wd{p}", tw), ("dve", tok_h[63] if ic == 0 else 0)]
                for j in range(4):
                    for eg in range(2):
                        tok_R = mm(pbR[j * 2 + eg][:, :], hb[ic][:, j * 128:(j + 1) * 128],
                                   bview(WDB_O + p * 1024, 1024)[:, eg * 512:(eg + 1) * 512],
                                   ic == 0, False,
                                   waits=(w0 if (j == 0 and eg == 0) else ()),
                                   inc="pe" if (j == 3 and eg == 1) else None)
                state[("wdb_free", qt, egh, ic)] = tok_R
            for hh in range(16):
                p = hh % 2
                wprev = (state.get(("wob_free", qt, egh, hh - 2), 0) if hh >= 2
                         else state.get(("wob_free", qt, egh - 1, 14 + hh), 0))
                tw = DMA("sync", f"wo{p}", bview(WOB_O + p * 1024, 1024), d["s_wo"][egh, hh],
                         waits=[("cvo", mark[("wo", hh)]), ("pe", wprev)])
                w0 = [(f"wo{p}", tw)]
                for j in range(4):
                    for eg in range(2):
                        tok_R = mm(pbR[j * 2 + eg][:, :], OTv[hh][:, j * 128:(j + 1) * 128],
                                   bview(WOB_O + p * 1024, 1024)[:, eg * 512:(eg + 1) * 512],
                                   False, hh == 15, waits=(w0 if (j == 0 and eg == 0) else ()),
                                   inc="pe" if (j == 3 and eg == 1) else None)
                state[("wob_free", qt, egh, hh)] = tok_R
            tev = None
            for j in range(4):
                p = j % 2
                xprev = (state.get(("xr_free", qt, egh, j - 2), 0) if j >= 2
                         else state.get(("xr_free", qt, egh - 1, 2 + j), 0))
                txr = DMA("sync", f"xr{p}", fview(XR_O + p * 1024, 1024),
                          d["x"][T0 + j * 128:T0 + (j + 1) * 128, egh * 1024:(egh + 1) * 1024],
                          waits=[("dve", xprev)])
                outs = fview(OUTS_O + j * 1024, 1024)
                for eg in range(2):
                    tev = dve(lambda e, o=outs[:, eg * 512:(eg + 1) * 512],
                              a=pbR[j * 2 + eg][:, :],
                              b=fview(XR_O + p * 1024, 1024)[:, eg * 512:(eg + 1) * 512]:
                              e.tensor_add(o, a, b), fast=True,
                              waits=[("pe", tok_R), (f"xr{p}", txr),
                                     (f"ot{j}", state.get(("out_prev", j), 0))])
                state[("xr_free", qt, egh, j)] = tev
                state[("out_prev", j)] = DMA(
                    "scalar", f"ot{j}", d["out"][T0 + j * 128:T0 + (j + 1) * 128, egh * 1024:(egh + 1) * 1024],
                    outs, waits=[("dve", tev)])
            state[("evac_done", qt, egh)] = tev
            if "c" in SUBJ:
                P.junction()

        state[("evac_prev",)] = tev
        state[("tokR_prev",)] = tok_R
        state[("ot_prev",)] = tuple(P.cnt[f"ot{j}"] for j in range(4))

        for g in reversed(pb67_g):
            g.__exit__(None, None, None)
        state["pbT_g"] = [nc.psum_tensor(f"pbT0_{qt + 1}", [128, 1024], BF16),
                          nc.psum_tensor(f"pbT1_{qt + 1}", [128, 1024], BF16)]
        state["pbT"] = [g.__enter__() for g in state["pbT_g"]]
        for k in (("pbT_free", 0), ("pbT_free", 1)):
            state[k] = 0

    for j in range(4):
        fin = P.cnt[f"ot{j}"]
        E("scalar", lambda e, s=f"ot{j}", v=fin: e.wait_ge(sems[s], v))
    for g in reversed(state["pbT_g"]):
        g.__exit__(None, None, None)

    with nc.Block() as blk:
        def runner(eng_name):
            drainy = eng_name in ("vector", "scalar", "gpsimd")
            own = OWNSEM.get(eng_name)

            def run(eng):
                c_own = 0
                for fn, ws, incs, is_dma in P.q[eng_name]:
                    own_inc = 0
                    if not is_dma:
                        for s, n in incs:
                            if s == own:
                                own_inc += n
                    for s, v in ws:
                        if s == own and v <= c_own and drainy:
                            continue
                        eng.wait_ge(sems[s], v)
                    inst = fn(eng)
                    if is_dma or not drainy:
                        for s, n in incs:
                            inst = inst.then_inc(sems[s], n)
                    else:
                        for s, n in incs:
                            eng.drain().then_inc(sems[s], n)
                    c_own += own_inc
            return run
        blk.sync(runner("sync"))
        blk.tensor(runner("tensor"))
        blk.scalar(runner("scalar"))
        blk.vector(runner("vector"))
        blk.gpsimd(runner("gpsimd"))

    stack.close()
    return nc


_CACHE = {}


def kernel(**inputs):
    x = np.asarray(inputs["L0_attn_x"])
    hw = _host_weights(inputs)
    ht = _host_tables()
    shared = {}
    shared.update(hw)
    shared.update(ht)
    nc = build_program()
    in_maps = []
    for b in range(8):
        m = dict(shared)
        m["x"] = np.ascontiguousarray(x[b])
        in_maps.append(m)
    res = run_bass_kernel_spmd(nc, in_maps, core_ids=list(range(8)))
    out = np.stack([res.results[b]["out"] for b in range(8)], axis=0)
    return out
